# version 18
# baseline (speedup 1.0000x reference)
import numpy as np
import ml_dtypes
import concourse.bass as bass
import concourse.mybir as mybir
from concourse.bass_utils import run_bass_kernel_spmd

F32 = mybir.dt.float32
BF16 = mybir.dt.bfloat16
AX = mybir.AxisListType.X
OP = mybir.AluOpType
AF = mybir.ActivationFunctionType

S, D = 2048, 1024
NCORES = 8
SEQ = 2
NE = 32
EPS = 1e-5

DBG = {"stop": None}

ENGINES = ("sync", "gpsimd", "tensor", "scalar", "vector")
COMPUTE = ("tensor", "scalar", "vector")


class Plan:
    def __init__(self):
        self.ops = {e: [] for e in ENGINES}
        self.cnt = {}
        self.events = {}

    def op(self, eng, fn, waits=(), inc=None, inc_n=1, key=None):
        if inc is not None:
            c = self.cnt.get(inc, 0) + inc_n
            self.cnt[inc] = c
            if key is not None:
                assert key not in self.events, f"dup event {key}"
                self.events[key] = (inc, c)
        self.ops[eng].append((fn, tuple(waits), inc, inc_n))

    def alias(self, key, src):
        self.events[key] = self.events[src]

    def marker(self, key, sem):
        assert key not in self.events, key
        self.events[key] = (sem, self.cnt.get(sem, 0))

    def fence(self, engs=COMPUTE, dma=True):
        sems = {"tensor": "pe", "scalar": "sc", "vector": "ve"}
        cur = {e: ("pe" if e == "tensor" else ("sc" if e == "scalar" else "ve"),
                   self.cnt.get(sems[e], 0)) for e in engs}
        if dma is True:
            dmaw = [(sm_, c) for sm_, c in self.cnt.items()
                    if not sm_.startswith(("_", "dwc", "dws")) and sm_ not in
                    ("pe", "sc", "ve", "gp")]
        elif dma:
            dmaw = [(sm_, c) for sm_, c in self.cnt.items() if sm_ in dma]
        else:
            dmaw = []
        for e in engs:
            waits = [cur[o] for o in engs if o != e] + dmaw
            self.op(e, lambda _: None, waits=waits)

    def emit(self, eng_name, eng, semh):
        waited = {}
        for fn, waits, inc, inc_n in self.ops[eng_name]:
            need = {}
            for w in waits:
                if isinstance(w, str):
                    s, v = self.events[w]
                else:
                    s, v = w
                if v > need.get(s, 0):
                    need[s] = v
            for s, v in need.items():
                if waited.get(s, -1) < v:
                    eng.wait_ge(semh[s], v)
                    waited[s] = v
            instr = fn(eng)
            if inc is not None:
                if instr is None:
                    eng.nop().then_inc(semh[inc], inc_n)
                else:
                    instr.then_inc(semh[inc], inc_n)


class SbufMap:
    def __init__(self, nc, base, top):
        self.nc, self.base, self.top, self.cur = nc, base, top, base
        self.marks = {}

    def alloc(self, name, shape, dtype, align=64):
        nb = int(np.prod(shape[1:])) * (2 if dtype == BF16 else 4)
        off = (self.cur + align - 1) // align * align
        t = self.nc.alloc_sbuf_tensor_at(name, list(shape), dtype, offset=off)
        self.cur = off + nb
        assert self.cur <= self.top, f"SBUF overflow at {name}: {self.cur}>{self.top}"
        return t

    def mark(self, n):
        self.marks[n] = self.cur

    def reset(self, n):
        self.cur = self.marks[n]


def build_nc():
    nc = bass.Bass("TRN2", target_bir_lowering=False)

    xin = nc.dram_tensor("x", [SEQ, S, D], F32, kind="ExternalInput").ap()
    wq = nc.dram_tensor("wq_t", [D, D], F32, kind="ExternalInput").ap()
    wk = nc.dram_tensor("wk_t", [D, D], F32, kind="ExternalInput").ap()
    wv = nc.dram_tensor("wv_t", [D, D], F32, kind="ExternalInput").ap()
    wo = nc.dram_tensor("wo_t", [D, D], F32, kind="ExternalInput").ap()
    rw = nc.dram_tensor("rw_t", [D, NE], F32, kind="ExternalInput").ap()
    rb = nc.dram_tensor("rbias", [NE, 1], F32, kind="ExternalInput").ap()
    gw = nc.dram_tensor("gw_t", [NE, D, 128], F32, kind="ExternalInput").ap()
    uw = nc.dram_tensor("uw_t", [NE, D, 128], F32, kind="ExternalInput").ap()
    dw = nc.dram_tensor("dw_t", [NE, 128, D], F32, kind="ExternalInput").ap()
    sg = nc.dram_tensor("sg_t", [D, 512], F32, kind="ExternalInput").ap()
    su = nc.dram_tensor("su_t", [D, 512], F32, kind="ExternalInput").ap()
    sd = nc.dram_tensor("sd_t", [512, D], F32, kind="ExternalInput").ap()
    ctab = nc.dram_tensor("cos_t", [128, S], F32, kind="ExternalInput").ap()
    stab = nc.dram_tensor("sin_t", [128, S], F32, kind="ExternalInput").ap()
    ctlo = nc.dram_tensor("cos_lo", [128, S], F32, kind="ExternalInput").ap()
    stlo = nc.dram_tensor("sin_lo", [128, S], F32, kind="ExternalInput").ap()
    y = nc.dram_tensor("y", [D, SEQ * S], F32, kind="ExternalOutput").ap()
    gwc = nc.dram_tensor("gw_bf", [NE, 128, 1024], BF16, kind="Internal").ap()
    uwc = nc.dram_tensor("uw_bf", [NE, 128, 1024], BF16, kind="Internal").ap()
    dwc = nc.dram_tensor("dw_bf", [NE, 128, 1024], BF16, kind="Internal").ap()

    P = Plan()

    base = ((nc.sbuf_base + 63) // 64) * 64
    sm = SbufMap(nc, base, nc.sbuf_top)

    xn = sm.alloc("xn", [128, 8 * 2048], BF16)
    pv = sm.alloc("pv", [128, 8 * 2048], BF16)
    cosb = sm.alloc("cosb", [128, S], BF16)
    sinb = sm.alloc("sinb", [128, S], BF16)
    ident = sm.alloc("ident", [128, 128], BF16)
    trim = sm.alloc("trim", [128, 128], BF16)
    onesb = sm.alloc("onesb", [128, 128], BF16)
    ones2h = sm.alloc("ones2h", [128, 2], BF16)
    rwsb = sm.alloc("rwsb", [128, 8 * NE], BF16)
    stage = [sm.alloc(f"stage{i}", [128, 1024], BF16) for i in range(3)]
    rbsb = sm.alloc("rbsb", [NE, 1], F32)
    k2m = sm.alloc("k2m", [2, 8], F32)
    sm.mark("arena")

    x_sb = [sm.alloc(f"x_sb{i}", [128, D], BF16) for i in range(2)]
    xn_tm = [sm.alloc(f"xn_tm{i}", [128, D], BF16) for i in range(2)]
    bn6 = sm.alloc("bn6", [128, 16], F32)
    mv = sm.alloc("mv", [128, 8], F32)
    v_sb = sm.alloc("v_sb", [128, 16 * 1040], BF16)
    p_T = sm.alloc("p_T", [128, 16 * 512], BF16)
    q_ext = [sm.alloc(f"q_ext{i}", [65, S], BF16) for i in range(4)]
    k_ext = [sm.alloc(f"k_ext{i}", [65, S], BF16) for i in range(4)]
    raw = sm.alloc("raw", [128, S], BF16)
    swp = sm.alloc("swp", [128, S], BF16)
    t1b = sm.alloc("t1b", [128, S], BF16)
    sqb = t1b
    mrow = sm.alloc("mrow", [2, S], BF16)
    k2t = sm.alloc("k2t", [2, 8], F32)
    lrec = sm.alloc("lrec", [128, 512], BF16)
    lbc_sb = [sm.alloc(f"lbc{i}", [128, 512], BF16) for i in range(2)]
    pvn = sm.alloc("pvn", [65, S], BF16)
    wqk_sb = [sm.alloc(f"wqk{i}", [128, 2 * 8 * 128], BF16) for i in range(2)]
    wv_sb = sm.alloc("wv_sb", [128, 8 * 512], BF16)
    sm.reset("arena")

    h_sb = sm.alloc("h_sb", [128, NE * 512], BF16)
    sh_sb = sm.alloc("sh_sb", [128, 4 * 512], BF16)
    x_bf = [sm.alloc("x_bf0", [128, 4 * 1024], BF16)] * 2
    gu_sb = [sm.alloc(f"gu_sb{i}", [128, 2 * 1024], BF16) for i in range(3)]
    dw_sb = [sm.alloc(f"dw_sb{i}", [128, 1024], BF16) for i in range(3)]
    sgs = sm.alloc("sgs", [128, 8 * 512], BF16)
    sus = sm.alloc("sus", [128, 8 * 512], BF16)
    sds = sm.alloc("sds", [128, 4 * 1024], BF16)
    wos = sm.alloc("wos", [128, 8 * 1024], BF16)
    wbc_sb = [sm.alloc(f"wbc{i}", [128, 512], BF16) for i in range(2)]
    gs_sb = sm.alloc("gs_sb", [128, 512], BF16)
    tt_sb = sm.alloc("tt_sb", [128, 512], BF16)
    lem = sm.alloc("lem", [NE, S], BF16)
    wtm = sm.alloc("wtm", [128, 16 * NE], BF16)
    ssf = sm.alloc("ssf", [128, NE], F32)
    t8f = sm.alloc("t8f", [128, 8], F32)
    mskf = sm.alloc("mskf", [128, NE], F32)
    nmf = sm.alloc("nmf", [128, 8], F32)
    out_sb = [sm.alloc(f"out_sb{i}", [128, 512], F32) for i in range(2)]
    selw = sm.alloc("selw", [NE, NE * 128], BF16)
    wem = sm.alloc("wem", [NE, S], BF16)

    ps4 = nc.alloc_psum_tensor("ps4", [128, 2048], F32)
    pvb = [nc.alloc_psum_tensor(f"pvb{i}", [128, 512], F32) for i in range(2)]
    pq2 = nc.alloc_psum_tensor("pq2", [128, 512], F32)
    plb = nc.alloc_psum_tensor("plb", [128, 512], F32)

    def s4(i):
        return ps4[:, (i % 4) * 512:(i % 4 + 1) * 512]

    WC = {"idx": 0, "jobs": []}

    def allw():
        return [(sm_, c) for sm_, c in P.cnt.items()
                if not sm_.startswith(("_", "dwc", "dws"))]

    def dma(engq, out, in_, waits=(), key=None, sem=None):
        sem = sem or ("dg" if engq == "gpsimd" else "dx")
        P.op(engq, lambda e, o=out, i=in_: e.dma_start(out=o, in_=i),
             waits=waits, inc=sem, inc_n=16, key=key)

    def emit_init():
        def g_init(e):
            if DBG.get("stop"):
                for t in (v_sb, pv, p_T, xn):
                    e.memset(t[:, :], 0.0)
                e.drain()
            e.memset(ident[:, :], 1.0)
            e.memset(trim[:, :], 1.0)
            e.memset(ones2h[:, :], 1.0)
            e.memset(onesb[:, :], 1.0)
            e.drain()
            e.affine_select(out=ident[:, :], in_=ident[:, :],
                            compare_op=OP.is_equal, fill=0.0, base=0,
                            pattern=[[-1, 128]], channel_multiplier=1)
            e.affine_select(out=trim[:, :], in_=trim[:, :],
                            compare_op=OP.is_ge, fill=0.0, base=0,
                            pattern=[[1, 128]], channel_multiplier=-1)
            e.affine_select(out=ones2h[:, :], in_=ones2h[:, :],
                            compare_op=OP.is_ge, fill=0.0, base=0,
                            pattern=[[-64, 2]], channel_multiplier=1)
            e.drain()
            r = e.affine_select(out=ones2h[:, :], in_=ones2h[:, :],
                                compare_op=OP.is_ge, fill=0.0, base=63,
                                pattern=[[64, 2]], channel_multiplier=-1)
            return r
        P.op("gpsimd", g_init, inc="gp", inc_n=1, key="init_gp")
        dma("gpsimd", cosb[:, :], ctab, key="ld_cos", sem="dcs")
        dma("gpsimd", sinb[:, :], stab, key="ld_sin", sem="dcs")
        dma("gpsimd", raw[:, :], ctlo, key="ld_clo", sem="dcs")
        dma("gpsimd", swp[:, :], stlo, key="ld_slo", sem="dcs")
        P.op("vector", lambda e: e.tensor_add(cosb[:, :], cosb[:, :], raw[:, :]),
             waits=["ld_slo"], inc="ve", inc_n=1, key="cs_add0")
        P.op("vector", lambda e: e.tensor_add(sinb[:, :], sinb[:, :], swp[:, :]),
             waits=["ld_slo"], inc="ve", inc_n=1, key="cs_add1")
        dma("sync", rbsb[:, :], rb, key="ld_rb", sem="dini")
        dma("gpsimd", rwsb[:, :].rearrange("p (k e) -> p k e", k=8),
            rw.rearrange("(k p) e -> p k e", p=128), key="ld_rw", sem="dinig")
        for ei in range(NE):
            for (src, dst) in ((gw, gwc), (uw, uwc), (dw, dwc)):
                WC["jobs"].append((src, dst, ei))

    def drip_wc(n):
        for _ in range(n):
            idx = WC["idx"]
            if idx >= len(WC["jobs"]):
                return
            WC["idx"] = idx + 1
            src, dst, ei = WC["jobs"][idx]
            sb = stage[idx % 3]
            w = [f"wc_st_{idx - 3}"] if idx >= 3 else []
            if dst is dwc:
                in_ap = src[ei]
                out_ap = sb[:, :]
            else:
                in_ap = src[ei].rearrange("(k p) h -> p k h", p=128)
                out_ap = sb[:, :].rearrange("p (k h) -> p k h", k=8)
            P.op("gpsimd", lambda e, o=out_ap, i=in_ap: e.dma_start(out=o, in_=i),
                 waits=w, inc=f"dwc{idx % 3}", inc_n=16, key=f"wc_ld_{idx}")
            P.op("sync", lambda e, sb=sb, d=dst[ei]: e.dma_start(out=d, in_=sb[:, :]),
                 waits=[f"wc_ld_{idx}"], inc=f"dws{idx % 3}", inc_n=16,
                 key=f"wc_st_{idx}")

    def emit_ln(s):
        for i in range(16):
            xw = [f"ln_ve_{s}_{i - 2}"] if i >= 2 else (allw() if s > 0 else [])
            dma("gpsimd", x_sb[i % 2][:, :], xin[s, i * 128:(i + 1) * 128, :],
                waits=xw, key=f"ln_x_{s}_{i}", sem=f"dx{i % 2}")
            if s == 0:
                drip_wc(1)

            def stats(e, i=i):
                xs = x_sb[i % 2]
                e.bn_stats(bn6[:, 0:6], xs[:, 0:512])
                e.bn_stats(bn6[:, 6:12], xs[:, 512:1024])
                means = bass.AP(bn6, 1, [[16, 128], [3, 4]])
                m2s = bass.AP(bn6, 2, [[16, 128], [3, 4]])
                def nops(n=4):
                    e.drain()
                nops()
                e.reduce_sum(mv[:, 0:1], means, axis=AX)
                e.reduce_sum(mv[:, 1:2], m2s, axis=AX)
                e.scalar_tensor_tensor(bn6[:, 12:16], means, 1.0, means,
                                       op0=OP.mult, op1=OP.mult)
                nops()
                e.reduce_sum(mv[:, 2:3], bn6[:, 12:16], axis=AX)
                e.tensor_scalar(mv[:, 3:4], mv[:, 1:2], 1.0 / D, None,
                                op0=OP.mult)
                nops()
                e.tensor_scalar(mv[:, 4:5], mv[:, 2:3], 0.25, mv[:, 3:4],
                                op0=OP.mult, op1=OP.add)
                e.scalar_tensor_tensor(mv[:, 5:6], mv[:, 0:1], -1.0 / 16.0,
                                       mv[:, 0:1], op0=OP.mult, op1=OP.mult)
                nops()
                e.tensor_scalar(mv[:, 4:5], mv[:, 4:5], mv[:, 5:6], EPS,
                                op0=OP.add, op1=OP.add)
                e.tensor_scalar(mv[:, 0:1], mv[:, 0:1], 0.25, None,
                                op0=OP.mult)
                nops()
                return e.reciprocal(mv[:, 5:6], mv[:, 4:5])
            sw = [f"ln_x_{s}_{i}"] + ([f"ln_sq_{s}_{i - 1}"] if i >= 1 else [])
            P.op("vector", stats, waits=sw, inc="ve", inc_n=1, key=f"ln_st_{s}_{i}")
            P.op("scalar", lambda e: e.activation(mv[:, 6:7], mv[:, 5:6], AF.Sqrt),
                 waits=[f"ln_st_{s}_{i}"], inc="sc", inc_n=1, key=f"ln_sq_{s}_{i}")
            vw = [f"ln_sq_{s}_{i}"] + ([f"ln_tr_{s}_{i - 2}"] if i >= 2 else [])
            P.op("vector", lambda e, i=i: e.tensor_scalar(
                xn_tm[i % 2][:, :], x_sb[i % 2][:, :], mv[:, 0:1], mv[:, 6:7],
                op0=OP.subtract, op1=OP.mult),
                waits=vw, inc="ve", inc_n=1, key=f"ln_ve_{s}_{i}")

            def trans(e, i=i):
                r = None
                for dc in range(8):
                    r = e.matmul(ps4[:, (dc // 4) * 512 + (dc % 4) * 128:
                                      (dc // 4) * 512 + (dc % 4 + 1) * 128],
                                 xn_tm[i % 2][:, dc * 128:(dc + 1) * 128],
                                 ident[:, :], start=True, stop=True)
                return r
            tw = [f"ln_ve_{s}_{i}", "init_gp"] + \
                 ([f"ln_ev_{s}_{i - 1}_1"] if i >= 1 else [])
            P.op("tensor", trans, waits=tw, inc="pe", inc_n=1, key=f"ln_tr_{s}_{i}")
            for b in range(2):
                def evac(e, i=i, b=b):
                    o = bass.AP(xn, (4 * b) * 2048 + i * 128,
                                [[8 * 2048, 128], [2048, 4], [1, 128]])
                    return e.copy(o, bass.AP(ps4, b * 512, [[2048, 128], [128, 4], [1, 128]]))
                P.op("scalar", evac, waits=[f"ln_tr_{s}_{i}"],
                     inc="sc", inc_n=1, key=f"ln_ev_{s}_{i}_{b}")

    def emit_attn_group(s, g):
        dcp_lim = DBG.get("dcp_lim", 4)
        head_lim = DBG.get("head_lim", 2)
        v_lim = DBG.get("v_lim", 16)
        if g == 0:
            def ainit(e):
                e.memset(bass.AP(v_sb, 64,
                                 [[16 * 1040, 128], [1040, 16], [65, 16], [1, 1]]), 1.0)
                r = None
                for i in range(4):
                    r = e.memset(k_ext[i][64:65, :], -1.0)
                return r
            P.op("gpsimd", ainit, waits=allw(), inc="gp", inc_n=1,
                 key=f"ainit_{s}")
        def mk_vproj_units(g=g):
            vslice = wv.rearrange("(k p) o -> p k o",
                                  p=128)[:, :, g * 512:(g + 1) * 512]
            vw_w = [f"v_pe_done_{s}_{g - 1}"] if g > 0 else allw()
            P.op("gpsimd", lambda e: e.dma_start(
                out=wv_sb[:, :].rearrange("p (k o) -> p k o", k=8), in_=vslice),
                waits=vw_w, inc="dwv", inc_n=16, key=f"wv_ld_{s}_{g}")
            units = []
            for i in range(v_lim):
                def unit(i=i, g=g):
                    def vproj(e, i=i):
                        r = None
                        for kc in range(8):
                            r = e.matmul(pvb[1][:, :],
                                         xn[:, kc * 2048 + i * 128: kc * 2048 + (i + 1) * 128],
                                         wv_sb[:, kc * 512:(kc + 1) * 512],
                                         start=(kc == 0), stop=(kc == 7))
                        return r
                    vw = [f"wv_ld_{s}_{g}"]
                    if i >= 1:
                        vw += [f"v_ev_{s}_{g}_{i - 1}"]
                    elif g > 0:
                        vw += [f"q_ev_{s}_{g - 1}_3_3"]
                    P.op("tensor", vproj, waits=vw, inc="pe", inc_n=1,
                         key=f"v_pe_{s}_{g}_{i}")

                    def vevac(e, i=i, g=g):
                        o = bass.AP(v_sb, i * 1040 + (g * 8) * 65,
                                    [[16 * 1040, 128], [65, 8], [1, 64]])
                        ip = bass.AP(pvb[1], 0, [[512, 128], [64, 8], [1, 64]])
                        return e.tensor_copy(o, ip)
                    P.op("vector", vevac,
                         waits=[f"v_pe_{s}_{g}_{i}", f"ainit_{s}"],
                         inc="ve", inc_n=1, key=f"v_ev_{s}_{g}_{i}")
                    if s == 0:
                        drip_wc(1)
                units.append(unit)
            return units
        if g == 0:
            for u in mk_vproj_units():
                u()
        P.marker(f"v_pe_done_{s}_{g}", "pe")

        def prev_key(dcp, fmt, back=1):
            ddc = dcp - back
            ss, gg = s, g
            while ddc < 0:
                gg -= 1
                if gg < 0:
                    ss -= 1
                    gg = 1
                if ss < 0:
                    return None
                ddc += 4
            kk = fmt.format(s=ss, g=gg, dcp=ddc)
            return kk if kk in P.events else None

        def emit_prep(dcp):
            dc = 4 * g + dcp
            key = f"{s}_{g}_{dcp}"
            qeA, qeB = q_ext[(2 * dcp) % 4], q_ext[(2 * dcp + 1) % 4]
            keA, keB = k_ext[(2 * dcp) % 4], k_ext[(2 * dcp + 1) % 4]
            wqk = wqk_sb[dcp % 2]
            kslice = wk.rearrange("(k p) o -> p k o", p=128)[:, :, dc * 128:(dc + 1) * 128]
            qslice = wq.rearrange("(k p) o -> p k o", p=128)[:, :, dc * 128:(dc + 1) * 128]
            if dcp >= 2:
                ww = [f"q_pe_{s}_{g}_{dcp - 2}_3"]
            elif g > 0:
                ww = [f"q_pe_{s}_{g - 1}_{dcp + 2}_3"]
            else:
                ww = allw()
            P.op("gpsimd", lambda e, ks=kslice, wqk=wqk: e.dma_start(
                out=wqk[:, 0:1024].rearrange("p (k o) -> p k o", k=8), in_=ks),
                waits=ww, inc=f"dwqk{dcp % 2}", inc_n=16, key=f"wk_ld_{key}")
            P.op("gpsimd", lambda e, qs=qslice, wqk=wqk: e.dma_start(
                out=wqk[:, 1024:2048].rearrange("p (k o) -> p k o", k=8), in_=qs),
                waits=ww, inc=f"dwqk{dcp % 2}", inc_n=16, key=f"wq_ld_{key}")
            if s == 0:
                drip_wc(2)

            chunks = []
            for which in ("k", "q"):
                woff = 0 if which == "k" else 1024
                def norm_mm(nt, which=which):
                    if which == "k":
                        w = [f"k_sq_{key}_{nt}"] +                             ([f"k2r_{key}_{nt - 1}"] if nt else
                             ([prev_key(dcp, "mrow_{s}_{g}_{dcp}_3")] if
                              prev_key(dcp, "mrow_{s}_{g}_{dcp}_3") else []))
                        P.op("tensor", lambda e, nt=nt: e.matmul(
                            pq2[0:2, :], ones2h[:, :],
                            sqb[:, nt * 512:(nt + 1) * 512],
                            start=True, stop=True),
                            waits=w, inc="pe", inc_n=1, key=f"k2_pe_{key}_{nt}")
                        def k2red(e, nt=nt):
                            r = e.reduce_max(k2t[0:2, nt:nt + 1], pq2[0:2, :], axis=AX)
                            if nt == 3:
                                e.drain()
                                e.reduce_max(k2m[0:2, 0:1], k2t[0:2, 0:4], axis=AX)
                                e.drain()
                                r = e.tensor_scalar(k2m[0:2, 1:2], k2m[0:2, 0:1],
                                                    0.5, None, op0=OP.mult)
                            return r
                        P.op("vector", k2red, waits=[f"k2_pe_{key}_{nt}"],
                             inc="ve", inc_n=1, key=f"k2r_{key}_{nt}")
                    else:
                        mg = [prev_key(dcp, "qm_{s}_{g}_{dcp}_1")]                             if prev_key(dcp, "qm_{s}_{g}_{dcp}_1") else []
                        P.op("tensor", lambda e, nt=nt: e.matmul(
                            pq2[0:2, :], ones2h[:, :],
                            sqb[:, nt * 512:(nt + 1) * 512],
                            start=True, stop=True),
                            waits=[f"q_sq_{key}_{nt}", f"k2r_{key}_3"] +
                                  ([f"mrow_{key}_{nt - 1}"] if nt else []),
                            inc="pe", inc_n=1, key=f"q2_pe_{key}_{nt}")
                        P.op("scalar", lambda e, nt=nt: e.activation(
                            mrow[0:2, nt * 512:(nt + 1) * 512], pq2[0:2, :],
                            AF.Relu, bias=k2m[0:2, 1:2], scale=0.5),
                            waits=[f"q2_pe_{key}_{nt}"] + (mg if nt == 0 else []),
                            inc="sc", inc_n=1, key=f"mrow_{key}_{nt}")
                def proj_chunk(nt, which=which, woff=woff, wqk=wqk,
                               norm_mm=norm_mm):
                    def proj(e, nt=nt, woff=woff, wqk=wqk):
                        r = None
                        for kc in range(8):
                            r = e.matmul(pvb[1][:, :],
                                         wqk[:, woff + kc * 128: woff + (kc + 1) * 128],
                                         xn[:, kc * 2048 + nt * 512: kc * 2048 + (nt + 1) * 512],
                                         start=(kc == 0), stop=(kc == 7))
                        return r
                    pw = [f"wq_ld_{key}"]
                    if nt >= 1:
                        pw += [f"{which}_ev_{key}_{nt - 1}"]
                    elif which == "q":
                        pw += [f"k_ev_{key}_3"]
                    elif dcp == 0:
                        pw += [f"v_ev_{s}_{g}_15"]
                    else:
                        pkv = prev_key(dcp, "q_ev_{s}_{g}_{dcp}_3")
                        if pkv:
                            pw += [pkv]
                    if nt == 0:
                        if which == "q":
                            pw += [f"k_t1_{key}", f"k_t2_{key}", f"k2_pe_{key}_3",
                                   f"ksw_{key}_3"]
                        else:
                            for fmt in ("q_t1_{s}_{g}_{dcp}", "q_t2_{s}_{g}_{dcp}",
                                        "q2_pe_{s}_{g}_{dcp}_3", "qsw_{s}_{g}_{dcp}_3"):
                                pk = prev_key(dcp, fmt)
                                if pk:
                                    pw += [pk]
                    P.op("tensor", proj, waits=pw, inc="pe", inc_n=1,
                         key=f"{which}_pe_{key}_{nt}")
                    P.op("vector", lambda e, nt=nt: e.tensor_copy(
                        raw[:, nt * 512:(nt + 1) * 512], pvb[1][:, :]),
                        waits=[f"{which}_pe_{key}_{nt}"],
                        inc="ve", inc_n=1, key=f"{which}_ev_{key}_{nt}")
                    sqg = []
                    if nt == 0:
                        if which == "q":
                            sqg = [f"k_e3a_{key}", f"k_e3b_{key}"]
                        else:
                            for fmt in ("q_e3a_{s}_{g}_{dcp}", "q_e3b_{s}_{g}_{dcp}"):
                                pk = prev_key(dcp, fmt)
                                if pk:
                                    sqg += [pk]
                    P.op("vector", lambda e, nt=nt: e.tensor_mul(
                        sqb[:, nt * 512:(nt + 1) * 512],
                        raw[:, nt * 512:(nt + 1) * 512],
                        raw[:, nt * 512:(nt + 1) * 512]),
                        waits=[f"{which}_ev_{key}_{nt}"] + sqg,
                        inc="ve", inc_n=1, key=f"{which}_sq_{key}_{nt}")
                    if nt >= 1:
                        norm_mm(nt - 1)
                for nt in range(4):
                    chunks.append(lambda nt=nt, pc=proj_chunk: pc(nt))

                def rope_a(which=which, norm_mm=norm_mm, wqk=wqk):
                    norm_mm(3)
                    evd = f"{which}_ev_{key}_3"
                    swg = []
                    if which == "q":
                        swg = [f"k_e3a_{key}", f"k_e3b_{key}", f"k_odd_{key}"]
                    else:
                        for fmt in ("q_e3a_{s}_{g}_{dcp}", "q_e3b_{s}_{g}_{dcp}",
                                    "q_odd_{s}_{g}_{dcp}"):
                            pk = prev_key(dcp, fmt)
                            if pk:
                                swg.append(pk)
                    for bi, (d0, s0) in enumerate(((0, 32), (32, 0), (64, 96),
                                                   (96, 64))):
                        dma("sync", swp[d0:d0 + 32, :], raw[s0:s0 + 32, :],
                            waits=[evd] + swg, key=f"{which}sw_{key}_{bi}",
                            sem="dsw")
                    swd = [f"{which}sw_{key}_{bi}" for bi in range(4)]
                    P.op("vector", lambda e: e.tensor_mul(t1b[:, :], raw[:, :],
                                                          cosb[:, :]),
                         waits=[evd, "cs_add1",
                                f"{'k2' if which == 'k' else 'q2'}_pe_{key}_3"],
                         inc="ve", inc_n=1, key=f"{which}_t1_{key}")
                    P.op("vector", lambda e: e.tensor_mul(swp[:, :], swp[:, :],
                                                          sinb[:, :]),
                         waits=swd + ["cs_add1"], inc="ve", inc_n=1,
                         key=f"{which}_t2_{key}")

                def rope_b(which=which):
                    eA, eB = (keA, keB) if which == "k" else (qeA, qeB)
                    extw = [f"{which}_t1_{key}", f"{which}_t2_{key}"]
                    pk = prev_key(dcp, "sdpaA_{s}_{g}_{dcp}", back=2)
                    if pk:
                        extw += [pk, prev_key(dcp, "sdpaB_{s}_{g}_{dcp}", back=2)]
                    P.op("vector", lambda e, eA=eA: e.tensor_add(
                        eA[0:64, :], t1b[0:64, :], swp[0:64, :]),
                        waits=extw, inc="ve", inc_n=1, key=f"{which}_e3a_{key}")
                    odg = []
                    if which == "q":
                        odg = [f"k_odd_{key}"]
                    else:
                        pk = prev_key(dcp, "q_odd_{s}_{g}_{dcp}")
                        odg = [pk] if pk else []
                    P.op("vector", lambda e: e.tensor_add(
                        swp[64:128, :], t1b[64:128, :], swp[64:128, :]),
                        waits=extw + odg, inc="ve", inc_n=1,
                        key=f"{which}_e3b_{key}")
                    dma("sync", eB[0:64, :], swp[64:128, :],
                        waits=[f"{which}_e3b_{key}"], key=f"{which}_odd_{key}",
                        sem="dod")
                    if which == "q":
                        for hh, qe in ((0, qeA), (1, qeB)):
                            dma("sync", qe[64:65, :], mrow[hh:hh + 1, :],
                                waits=[f"mrow_{key}_3"], key=f"qm_{key}_{hh}",
                                sem="dqm")
                chunks.append(rope_a)
                chunks.append(rope_b)
            return chunks

        fill_state = {"q": [], "slot": 0, "pulled": 0, "npc": 12}

        def pull():
            fill_state["slot"] += 1
            while (fill_state["q"] and
                   fill_state["pulled"] * 24 < fill_state["slot"] *
                   fill_state["npc"]):
                fill_state["q"].pop(0)()
                fill_state["pulled"] += 1

        def emit_sdpa_pair(dcp):
            dc = 4 * g + dcp
            key = f"{s}_{g}_{dcp}"
            qeA, qeB = q_ext[(2 * dcp) % 4], q_ext[(2 * dcp + 1) % 4]
            keA, keB = k_ext[(2 * dcp) % 4], k_ext[(2 * dcp + 1) % 4]
            pending_lbc = []

            def flush_lbc():
                while pending_lbc:
                    fn, waits, keyx = pending_lbc.pop(0)
                    P.op("tensor", fn, waits=waits, inc="pe", inc_n=1, key=keyx)

            pending_vec = []

            def flush_vec():
                while pending_vec:
                    fn, waits, keyx = pending_vec.pop(0)
                    P.op("vector", fn, waits=waits, inc="ve", inc_n=1, key=keyx)

            pending_dma = []

            def flush_dma():
                while pending_dma:
                    out_, in_, waits, keyx = pending_dma.pop(0)
                    dma("sync", out_, in_, waits=waits, key=keyx, sem="dpv")

            chunks = []
            for hh in range(head_lim):
                h = 2 * dc + hh
                qe, ke = (qeA, keA) if hh == 0 else (qeB, keB)
                ready = [f"v_pe_done_{s}_{g}", "init_gp", f"ainit_{s}",
                         f"v_ev_{s}_{g}_15"]
                if hh == 0:
                    ready += [f"q_e3a_{key}", f"k_e3a_{key}", f"qm_{key}_1"]
                else:
                    ready += [f"q_odd_{key}", f"k_odd_{key}", f"qm_{key}_1"]

                def it_chunk(it, hh=hh, h=h, qe=qe, ke=ke, ready=ready):
                    if s == 0 and it == 0:
                        drip_wc(2)
                    qk_idx = 2 * it * (it + 1)
                    jmax = 4 * (it + 1)
                    tiles = []
                    for jc in range(jmax):
                        tiles.append((jc, qk_idx % 4, qk_idx))
                        qk_idx += 1
                    runs = []
                    cur = []
                    for (jc, slot, qi) in tiles:
                        if cur and (slot == 0 or len(cur) == 4):
                            runs.append(cur)
                            cur = []
                        cur.append((jc, slot, qi))
                    if cur:
                        runs.append(cur)
                    first_run = True
                    for run in runs:
                        for (jc, slot, qi) in run:
                            w = list(ready) if qi == 0 else []
                            if qi >= 4:
                                w += [f"expb_{s}_{h}_{qi - 4}"]
                            else:
                                if h > 0:
                                    w += [f"sdpa_expdone_{s}_{h - 1}"]
                                if dcp == 0 and hh == 0 and g == 0 and qi < 2:
                                    w += [f"ln_ev_{s}_15_{qi}"]
                            cq = 0
                            P.op("tensor", lambda e, jc=jc, slot=slot, it=it,
                                 qe=qe, ke=ke, cq=cq: e.matmul(
                                     ps4[:, slot * 512 + cq:(slot + 1) * 512],
                                     ke[0:65, jc * 128:(jc + 1) * 128],
                                     qe[0:65, it * 512 + cq:(it + 1) * 512],
                                     start=True, stop=True),
                                 waits=w, inc="pe", inc_n=1, key=f"qk_{s}_{h}_{qi}")
                        if first_run:
                            flush_lbc()
                            flush_vec()
                            flush_dma()
                            first_run = False
                        jc0, slot0, qi0 = run[0]
                        qiL = run[-1][2]
                        ew = [f"qk_{s}_{h}_{qiL}"]
                        if it > 0:
                            ew += [f"pv_{s}_{h}_{it - 1}"]
                        elif h > 0:
                            ew += [f"pv_{s}_{h - 1}_3"]
                        nrun = len(run)
                        P.op("scalar", lambda e, slot0=slot0, jc0=jc0, nrun=nrun:
                             e.activation(
                                 p_T[:, jc0 * 512: (jc0 + nrun) * 512],
                                 ps4[:, slot0 * 512: (slot0 + nrun) * 512],
                                 AF.Exp, scale=0.125),
                             waits=ew, inc="sc", inc_n=1, key=f"expb_{s}_{h}_{qi0}")
                        for (jc, slot, qi) in run[1:]:
                            P.alias(f"expb_{s}_{h}_{qi}", f"expb_{s}_{h}_{qi0}")
                        for (jc, slot, qi) in run:
                            if jc < 4 * it:
                                continue
                            c = 128 * jc - 512 * it
                            if c > 0:
                                P.op("vector", lambda e, jc=jc, c=c: e.memset(
                                    p_T[:, jc * 512: jc * 512 + c], 0.0),
                                    waits=[f"expb_{s}_{h}_{qi0}"],
                                    inc="ve", inc_n=1, key=f"pz_{s}_{h}_{qi}")
                            P.op("vector", lambda e, jc=jc, c=c: e.tensor_mul(
                                p_T[:, jc * 512 + c: jc * 512 + c + 128],
                                p_T[:, jc * 512 + c: jc * 512 + c + 128],
                                trim[:, :]),
                                waits=[f"expb_{s}_{h}_{qi0}"],
                                inc="ve", inc_n=1, key=f"msk_{s}_{h}_{qi}")
                        pull()
                    base_qi = qk_idx - jmax
                    for jc in range(jmax):
                        qi = base_qi + jc
                        c = max(0, 128 * jc - 512 * it)
                        w = [f"expb_{s}_{h}_{qi}"]
                        if jc >= 4 * it:
                            w += [f"msk_{s}_{h}_{qi}"]
                            if c > 0:
                                w += [f"pz_{s}_{h}_{qi}"]
                        if jc == 0:
                            if it >= 1:
                                w += [f"pvn_{s}_{h}_{it - 1}"]
                            elif h > 0:
                                w += [f"pvn_{s}_{h - 1}_3"]
                        P.op("tensor", lambda e, it=it, jc=jc, h=h, jmax=jmax: e.matmul(
                            pvb[0][0:65, :],
                            v_sb[:, jc * 1040 + h * 65: jc * 1040 + h * 65 + 65],
                            p_T[:, jc * 512: (jc + 1) * 512],
                            start=(jc == 0), stop=(jc == jmax - 1)),
                            waits=w, inc="pe", inc_n=1,
                            key=(f"pv_{s}_{h}_{it}" if jc == jmax - 1 else None))
                    lg = [f"pv_{s}_{h}_{it}"]
                    if it >= 1:
                        lg += [f"lbc_pe_{s}_{h}_{it - 1}"]
                    elif h > 0:
                        lg += [f"lbc_pe_{s}_{h - 1}_3"]
                    P.op("vector", lambda e, it=it: e.tensor_copy(
                        lrec[64:65, :], pvb[0][64:65, :]),
                        waits=lg, inc="ve", inc_n=1, key=f"lcp_{s}_{h}_{it}")
                    bw = [f"lcp_{s}_{h}_{it}"]
                    if it >= 1:
                        bw += [f"rcp_{s}_{h}_{it - 1}"]
                    elif h > 0:
                        bw += [f"rcp_{s}_{h - 1}_3"]
                    pending_lbc.append((
                        lambda e: e.matmul(plb[0:64, :], onesb[64:65, 0:64],
                                           lrec[64:65, :], start=True, stop=True),
                        bw, f"lbc_pe_{s}_{h}_{it}"))
                    rg = [f"lbc_pe_{s}_{h}_{it}"]
                    if it >= 2:
                        rg += [f"pvn_{s}_{h}_{it - 2}"]
                    elif h > 0:
                        rg += [f"pvn_{s}_{h - 1}_{it + 2}"]
                    pending_vec.append((
                        lambda e, it=it: e.reciprocal(
                            lbc_sb[it % 2][0:64, :], plb[0:64, :]),
                        rg, f"rcp_{s}_{h}_{it}"))
                    pg = [f"rcp_{s}_{h}_{it}"]
                    if it == 0 and h > 0:
                        pg += [f"pvdma_{s}_{h - 1}"]
                    pending_vec.append((
                        lambda e, it=it: e.tensor_mul(
                            pvn[0:64, it * 512:(it + 1) * 512],
                            pvb[0][0:64, :],
                            lbc_sb[it % 2][0:64, :]),
                        pg, f"pvn_{s}_{h}_{it}"))
                    if it == 3:
                        P.marker(f"sdpa_expdone_{s}_{h}", "sc")
                        pending_dma.append((
                            pv[(h % 2) * 64:(h % 2) * 64 + 64,
                               (h // 2) * 2048:(h // 2) * 2048 + 2048],
                            pvn[0:64, :],
                            [f"pvn_{s}_{h}_{i2}" for i2 in range(4)],
                            f"pvdma_{s}_{h}"))
                        P.marker(f"sdpaA_{key}" if hh == 0 else f"sdpaB_{key}",
                                 "pe")

                for it in range(4):
                    chunks.append(lambda it=it, ic=it_chunk: ic(it))
            chunks.append(lambda: (flush_lbc(), flush_vec(), flush_dma()))
            return chunks

        for step in range(dcp_lim + 1):
            if step < dcp_lim:
                pchunks = emit_prep(step)
            elif g == 0 and step == 4:
                pchunks = mk_vproj_units(g + 1)
            else:
                pchunks = []
            schunks = emit_sdpa_pair(step - 1) if step >= 1 else []
            if not schunks:
                for c in pchunks:
                    c()
                continue
            fill_state["q"] = list(pchunks)
            fill_state["slot"] = 0
            fill_state["pulled"] = 0
            fill_state["npc"] = max(len(pchunks), 1)
            for c in schunks:
                c()
            for c in fill_state["q"]:
                c()
            fill_state["q"] = []

    def emit_ffn(s):
        drip_wc(len(WC["jobs"]))
        shw = allw()
        P.op("gpsimd", lambda e: e.dma_start(
            out=sgs[:, :].rearrange("p (k o) -> p k o", k=8),
            in_=sg.rearrange("(k p) o -> p k o", p=128)),
            waits=shw, inc="dsh", inc_n=16, key=f"sg_ld_{s}")
        P.op("gpsimd", lambda e: e.dma_start(
            out=sus[:, :].rearrange("p (k o) -> p k o", k=8),
            in_=su.rearrange("(k p) o -> p k o", p=128)),
            waits=shw, inc="dsh", inc_n=16, key=f"su_ld_{s}")
        P.op("gpsimd", lambda e: e.dma_start(
            out=sds[:, :].rearrange("p (k o) -> p k o", k=4),
            in_=sd.rearrange("(k p) o -> p k o", p=128)),
            waits=shw, inc="dsh", inc_n=16, key=f"sd_ld_{s}")
        P.op("gpsimd", lambda e: e.dma_start(
            out=wos[:, :].rearrange("p (k o) -> p k o", k=8),
            in_=wo.rearrange("(k p) o -> p k o", p=128)),
            waits=shw, inc="dsh", inc_n=16, key=f"wo_ld_{s}")

        def selinit(e):
            e.memset(selw[:, :], 1.0)
            e.drain()
            return e.affine_select(out=selw[:, :], in_=selw[:, :],
                                   compare_op=OP.is_equal, fill=0.0, base=0,
                                   pattern=[[-1, NE], [0, 128]],
                                   channel_multiplier=1)
        P.op("gpsimd", selinit, waits=allw(), inc="gp", inc_n=1, key=f"selw_{s}")
        P.fence(dma=("dx0", "dx1", "dwv", "dwqk0", "dwqk1", "dsw", "dod", "dqm", "dpv"))
        for nt in range(4):
            def lgm(e, nt=nt):
                r = None
                for kc in range(8):
                    r = e.matmul(pvb[nt % 2][0:NE, :],
                                 rwsb[:, kc * NE:(kc + 1) * NE],
                                 xn[:, kc * 2048 + nt * 512: kc * 2048 + (nt + 1) * 512],
                                 start=(kc == 0), stop=(kc == 7))
                return r
            P.op("tensor", lgm, waits=["ld_rw"] +
                 ([f"lg_ev_{s}_{nt - 2}"] if nt >= 2 else []),
                 inc="pe", inc_n=1, key=f"lg_pe_{s}_{nt}")
            P.op("vector", lambda e, nt=nt: e.tensor_scalar(
                lem[0:NE, nt * 512:(nt + 1) * 512],
                pvb[nt % 2][0:NE, :],
                rbsb[0:NE, 0:1], None, op0=OP.add),
                waits=[f"lg_pe_{s}_{nt}", "ld_rb"], inc="ve", inc_n=1,
                key=f"lg_ev_{s}_{nt}")
        bank = (lambda i: pq2 if i % 2 == 0 else plb)
        for i in range(16):
            tw = [f"lg_ev_{s}_3"] + ([f"wem_{s}_{i - 2}"] if i >= 2 else [])
            P.op("tensor", lambda e, i=i, bank=bank: e.matmul(
                bank(i)[:, 0:NE], lem[0:NE, i * 128:(i + 1) * 128],
                ident[0:NE, 0:NE], start=True, stop=True),
                waits=tw, inc="pe", inc_n=1, key=f"rt_pe_{s}_{i}")
            P.op("vector", lambda e, i=i, bank=bank: e.reduce_max(
                nmf[:, 0:1], bank(i)[:, 0:NE], axis=AX, negate=True),
                waits=[f"rt_pe_{s}_{i}"] + ([f"rt_w_{s}_{i - 1}"] if i else []),
                inc="ve", inc_n=1, key=f"rt_nm_{s}_{i}")
            P.op("scalar", lambda e, i=i, bank=bank: e.activation(
                ssf[:, :], bank(i)[:, 0:NE], AF.Exp, bias=nmf[:, 0:1]),
                waits=[f"rt_nm_{s}_{i}"] + ([f"rt_w_{s}_{i - 1}"] if i else []),
                inc="sc", inc_n=1, key=f"rt_exp_{s}_{i}")

            def rmath(e, i=i):
                def nops(n=4):
                    e.drain()
                e.max(out=t8f[:, :], in_=ssf[:, :])
                nops()
                e.reduce_sum(nmf[:, 1:2], t8f[:, 0:4], axis=AX)
                e.tensor_scalar(mskf[:, :], ssf[:, :], t8f[:, 3:4], None, op0=OP.is_ge)
                nops()
                e.reciprocal(nmf[:, 2:3], nmf[:, 1:2])
                nops()
                return e.scalar_tensor_tensor(
                    wtm[:, i * NE:(i + 1) * NE], ssf[:, :], nmf[:, 2:3], mskf[:, :],
                    op0=OP.mult, op1=OP.mult)
            P.op("vector", rmath, waits=[f"rt_exp_{s}_{i}"],
                 inc="ve", inc_n=1, key=f"rt_w_{s}_{i}")
            P.op("tensor", lambda e, i=i, bank=bank: e.matmul(
                bank(i)[0:NE, 128:256], wtm[:, i * NE:(i + 1) * NE], ident[:, :],
                start=True, stop=True),
                waits=[f"rt_w_{s}_{i}"] + ([f"wem_{s}_{i - 2}"] if i >= 2 else []),
                inc="pe", inc_n=1, key=f"rt_tp_{s}_{i}")
            P.op("scalar", lambda e, i=i, bank=bank: e.copy(
                wem[0:NE, i * 128:(i + 1) * 128], bank(i)[0:NE, 128:256]),
                waits=[f"rt_tp_{s}_{i}"], inc="sc", inc_n=1, key=f"wem_{s}_{i}")
        P.fence(dma=("dx0", "dx1", "dwv", "dwqk0", "dwqk1", "dsw", "dod", "dqm", "dpv"))

        for nt in range(4):
            k = f"{s}_{nt}"
            pk = f"{s}_{nt - 1}" if nt > 0 else (f"{s - 1}_3" if s > 0 else None)
            P.op("gpsimd", lambda e, nt=nt: e.dma_start(
                out=x_bf[nt % 2][:, :].rearrange("p (a d) -> p a d", a=4),
                in_=xin[s, nt * 512:(nt + 1) * 512, :].rearrange(
                    "(a p) d -> p a d", p=128)),
                waits=([f"ds_pe_{s}_{nt - 1}_7"] if nt >= 1 else allw()),
                inc="dxb", inc_n=16, key=f"xbf_{k}")
            for hc in range(4):
                def sgm(e, hc=hc, nt=nt):
                    r = None
                    for kc in range(8):
                        r = e.matmul(ps4[:, (hc % 2) * 512:(hc % 2) * 512 + 512],
                                     sgs[:, kc * 512 + hc * 128: kc * 512 + (hc + 1) * 128],
                                     xn[:, kc * 2048 + nt * 512: kc * 2048 + (nt + 1) * 512],
                                     start=(kc == 0), stop=(kc == 7))
                    return r
                sgw = [f"wo_ld_{s}"] + ([f"ssil_{k}_{hc - 2}"] if hc >= 2 else [])
                if hc < 2 and pk:
                    sgw += [f"oev_{pk}_{hc}"]
                P.op("tensor", sgm, waits=sgw,
                     inc="pe", inc_n=1, key=f"sg_pe_{k}_{hc}")

                def sum_(e, hc=hc, nt=nt):
                    r = None
                    for kc in range(8):
                        r = e.matmul(ps4[:, 1024 + (hc % 2) * 512: 1024 + (hc % 2) * 512 + 512],
                                     sus[:, kc * 512 + hc * 128: kc * 512 + (hc + 1) * 128],
                                     xn[:, kc * 2048 + nt * 512: kc * 2048 + (nt + 1) * 512],
                                     start=(kc == 0), stop=(kc == 7))
                    return r
                suw = [f"wo_ld_{s}"] + ([f"shm_{k}_{hc - 2}"] if hc >= 2 else [])
                if hc < 2 and pk:
                    suw += [f"oev_{pk}_{2 + hc}"]
                P.op("tensor", sum_, waits=suw,
                     inc="pe", inc_n=1, key=f"su_pe_{k}_{hc}")
                slw = [f"sg_pe_{k}_{hc}"] + ([f"shm_{k}_{hc - 1}"] if hc else [])
                if hc == 0 and pk:
                    slw += [f"tm_{pk}_31"]
                P.op("scalar", lambda e, hc=hc: e.activation(
                    gs_sb[:, :], ps4[:, (hc % 2) * 512:(hc % 2) * 512 + 512], AF.Silu),
                    waits=slw, inc="sc", inc_n=1, key=f"ssil_{k}_{hc}")
                P.op("vector", lambda e, hc=hc: e.tensor_mul(
                    sh_sb[:, hc * 512:(hc + 1) * 512], gs_sb[:, :],
                    ps4[:, 1024 + (hc % 2) * 512: 1024 + (hc % 2) * 512 + 512]),
                    waits=[f"ssil_{k}_{hc}", f"su_pe_{k}_{hc}"],
                    inc="ve", inc_n=1, key=f"shm_{k}_{hc}")
            for ei in range(NE):
                gu = gu_sb[ei % 3]
                lw = []
                if ei >= 3:
                    lw += [f"u_pe_{k}_{ei - 3}"]
                elif pk:
                    lw += [f"u_pe_{pk}_{NE - 3 + ei}"]
                elif nt == 0:
                    lw += allw()
                P.op("sync", lambda e, gu=gu, ei=ei: e.dma_start(
                    out=gu[:, 0:1024], in_=gwc[ei]),
                    waits=lw + [f"wc_st_{3 * ei}"],
                    inc=f"dgu{ei % 3}", inc_n=16, key=f"g_ld_{k}_{ei}")
                P.op("sync", lambda e, gu=gu, ei=ei: e.dma_start(
                    out=gu[:, 1024:2048], in_=uwc[ei]),
                    waits=lw + [f"wc_st_{3 * ei + 1}"],
                    inc=f"dgu{ei % 3}", inc_n=16, key=f"u_ld_{k}_{ei}")
                wbank = pq2 if ei % 2 == 0 else plb
                wpw = [f"selw_{s}"] + ([f"wbc_ev_{k}_{ei - 2}"] if ei >= 2 else [])
                if ei < 2 and pk:
                    wpw += [f"oev_{pk}_{6 + ei}"]
                P.op("tensor", lambda e, ei=ei, nt=nt, wbank=wbank: e.matmul(
                    wbank[:, :],
                    selw[0:NE, ei * 128:(ei + 1) * 128],
                    wem[0:NE, nt * 512:(nt + 1) * 512], start=True, stop=True),
                    waits=wpw, inc="pe", inc_n=1, key=f"wbc_pe_{k}_{ei}")
                P.op("scalar", lambda e, ei=ei, wbank=wbank: e.copy(
                    wbc_sb[ei % 2][:, :], wbank[:, :]),
                    waits=[f"wbc_pe_{k}_{ei}"] +
                          ([f"hm_{k}_{ei - 2}"] if ei >= 2 else
                           ([f"hm_{pk}_{30 + ei}"] if pk else [])),
                    inc="sc", inc_n=1, key=f"wbc_ev_{k}_{ei}")

                gb = [ps4[:, 0:512], ps4[:, 512:1024], pvb[0][:, :]][ei % 3]
                ub = [ps4[:, 1024:1536], ps4[:, 1536:2048], pvb[1][:, :]][ei % 3]

                def gmm(e, gu=gu, nt=nt, ei=ei, gb=gb):
                    r = None
                    for kc in range(8):
                        r = e.matmul(gb, gu[:, kc * 128:(kc + 1) * 128],
                                     xn[:, kc * 2048 + nt * 512: kc * 2048 + (nt + 1) * 512],
                                     start=(kc == 0), stop=(kc == 7))
                    return r
                gw_ = [f"u_ld_{k}_{ei}"]
                if ei >= 3:
                    gw_ += [f"gsil_{k}_{ei - 3}"]
                elif ei % 3 < 2:
                    gw_ += [f"ssil_{k}_{(ei % 3) + 2}"]
                elif pk:
                    gw_ += [f"oev_{pk}_4"]
                P.op("tensor", gmm, waits=gw_, inc="pe", inc_n=1, key=f"g_pe_{k}_{ei}")

                def umm(e, gu=gu, nt=nt, ei=ei, ub=ub):
                    r = None
                    for kc in range(8):
                        r = e.matmul(ub, gu[:, 1024 + kc * 128: 1024 + (kc + 1) * 128],
                                     xn[:, kc * 2048 + nt * 512: kc * 2048 + (nt + 1) * 512],
                                     start=(kc == 0), stop=(kc == 7))
                    return r
                uw_ = [f"u_ld_{k}_{ei}"]
                if ei >= 3:
                    uw_ += [f"tm_{k}_{ei - 3}"]
                elif ei % 3 < 2:
                    uw_ += [f"shm_{k}_{(ei % 3) + 2}"]
                elif pk:
                    uw_ += [f"oev_{pk}_5"]
                P.op("tensor", umm, waits=uw_, inc="pe", inc_n=1, key=f"u_pe_{k}_{ei}")
                P.op("scalar", lambda e, ei=ei, gb=gb: e.activation(
                    gs_sb[:, :], gb, AF.Silu),
                    waits=[f"g_pe_{k}_{ei}"] +
                          ([f"tm_{k}_{ei - 1}"] if ei else [f"shm_{k}_3"]),
                    inc="sc", inc_n=1, key=f"gsil_{k}_{ei}")
                tmw = [f"gsil_{k}_{ei}", f"u_pe_{k}_{ei}"]
                if ei:
                    tmw += [f"hm_{k}_{ei - 1}"]
                elif pk:
                    tmw += [f"hm_{pk}_31"]
                P.op("vector", lambda e, ei=ei, ub=ub: e.tensor_mul(
                    tt_sb[:, :], gs_sb[:, :], ub),
                    waits=tmw, inc="ve", inc_n=1, key=f"tm_{k}_{ei}")
                hw = [f"tm_{k}_{ei}", f"wbc_ev_{k}_{ei}"]
                P.op("vector", lambda e, ei=ei: e.tensor_mul(
                    h_sb[:, ei * 512:(ei + 1) * 512], tt_sb[:, :], wbc_sb[ei % 2][:, :]),
                    waits=hw, inc="ve", inc_n=1, key=f"hm_{k}_{ei}")
            def dbank(dc, lo=0, hi=512):
                if dc < 4:
                    return ps4[:, dc * 512 + lo: dc * 512 + hi]
                if dc < 6:
                    return pvb[dc - 4][:, lo:hi]
                return (pq2 if dc == 6 else plb)[:, lo:hi]
            bank_free = {0: f"gsil_{k}_30", 1: f"gsil_{k}_31",
                         2: f"tm_{k}_30", 3: f"tm_{k}_31",
                         4: f"gsil_{k}_29", 5: f"tm_{k}_29",
                         6: f"wbc_ev_{k}_30", 7: f"wbc_ev_{k}_31"}
            for dc in range(8):
                def womm(e, dc=dc, nt=nt):
                    r = None
                    for kc in range(8):
                        r = e.matmul(dbank(dc, 0, 512),
                                     wos[:, kc * 1024 + dc * 128: kc * 1024 + (dc + 1) * 128],
                                     pv[:, kc * 2048 + nt * 512: kc * 2048 + (nt + 1) * 512],
                                     start=(kc == 0), stop=False,
                                     skip_group_check=True)
                    return r
                ww = [f"wo_ld_{s}", f"pvdma_{s}_15", bank_free[dc]]
                if nt > 0:
                    ww += [f"oev_{s}_{nt - 1}_{dc}"]
                elif s > 0:
                    ww += [f"oev_{s - 1}_3_{dc}"]
                P.op("tensor", womm, waits=ww, inc="pe", inc_n=1, key=f"wo_pe_{k}_{dc}")
            for ei in range(NE):
                dwb = dw_sb[ei % 3]
                lw = [f"wc_st_{3 * ei + 2}"]
                if ei >= 3:
                    lw += [f"dse_{k}_{ei - 3}"]
                elif pk:
                    lw += [f"dse_{pk}_{NE - 3 + ei}"]
                elif nt == 0:
                    lw += allw()
                P.op("sync", lambda e, dwb=dwb, ei=ei: e.dma_start(
                    out=dwb[:, :], in_=dwc[ei]),
                    waits=lw, inc=f"ddw{ei % 3}", inc_n=16, key=f"d_ld_{k}_{ei}")
                for dc in range(8):
                    P.op("tensor", lambda e, dc=dc, ei=ei, dwb=dwb: e.matmul(
                        dbank(dc, 0, 512), dwb[:, dc * 128:(dc + 1) * 128],
                        h_sb[:, ei * 512:(ei + 1) * 512], start=False, stop=False,
                        skip_group_check=True),
                        waits=([f"d_ld_{k}_{ei}", f"hm_{k}_{ei}"] if dc == 0 else []),
                        inc="pe", inc_n=1,
                        key=(f"dse_{k}_{ei}" if dc == 7 else None))
            for dc in range(8):
                for a in range(4):
                    P.op("tensor", lambda e, dc=dc, a=a, nt=nt: e.matmul(
                        dbank(dc, a * 128, (a + 1) * 128),
                        x_bf[nt % 2][:, a * 1024 + dc * 128: a * 1024 + (dc + 1) * 128],
                        ident[:, :], start=False, stop=False,
                        skip_group_check=True),
                        waits=([f"xbf_{k}"] if (dc == 0 and a == 0) else []),
                        inc="pe", inc_n=1)
            for hc in range(4):
                for dc in range(8):
                    P.op("tensor", lambda e, dc=dc, hc=hc: e.matmul(
                        dbank(dc, 0, 512), sds[:, hc * 1024 + dc * 128: hc * 1024 + (dc + 1) * 128],
                        sh_sb[:, hc * 512:(hc + 1) * 512], start=False,
                        stop=(hc == 3), skip_group_check=True),
                        waits=([f"wo_ld_{s}", f"shm_{k}_{hc}"] if dc == 0 else []),
                        inc="pe", inc_n=1,
                        key=(f"ds_pe_{k}_{dc}" if hc == 3 else None))
            for dc in range(8):
                ow = [f"ds_pe_{k}_{dc}"]
                if dc >= 2:
                    ow += [f"ost_{k}_{dc - 2}"]
                elif pk:
                    ow += [f"ost_{pk}_{dc + 6}"]
                P.op("vector", lambda e, dc=dc: e.tensor_copy(
                    out_sb[dc % 2][:, :], dbank(dc, 0, 512)),
                    waits=ow, inc="ve", inc_n=1, key=f"oev_{k}_{dc}")
                dma("sync", y[dc * 128:(dc + 1) * 128,
                              s * 2048 + nt * 512: s * 2048 + (nt + 1) * 512],
                    out_sb[dc % 2][:, :], waits=[f"oev_{k}_{dc}"],
                    sem=f"dout{dc % 2}", key=f"ost_{k}_{dc}")
        P.marker(f"b4_pe_done_{s}", "pe")

    dumps = []

    def dbg_dump(tag, t, rows, cols, dtype):
        d = nc.dram_tensor(f"dbg_{tag}", [rows, cols], dtype,
                           kind="ExternalOutput").ap()
        w = [(sm_, c) for sm_, c in P.cnt.items() if not sm_.startswith("_")]
        dma("sync", d, t[0:rows, 0:cols], waits=w, key=f"dbg_{tag}", sem="ddbg")
        dumps.append(tag)

    stop = DBG.get("stop")
    emit_init()
    done = False
    for s in range(SEQ):
        if done:
            break
        P.fence()
        emit_ln(s)
        if stop == f"ln{s}":
            dbg_dump(f"xn{s}", xn, 128, 8 * 2048, BF16)
            dbg_dump(f"bn{s}", bn6, 128, 16, F32)
            dbg_dump(f"mv{s}", mv, 128, 8, F32)
            dbg_dump(f"xsb{s}", x_sb[1], 128, 1024, BF16)
            done = True
            break
        P.fence()
        for g in range(2):
            emit_attn_group(s, g)
            if stop == f"attn{s}{g}":
                dbg_dump(f"vsb{s}{g}", v_sb, 128, 16 * 1040, BF16)
                dbg_dump(f"pv{s}{g}", pv, 128, 8 * 2048, BF16)
                done = True
                break
        if done:
            break
        if stop == f"rt{s}":
            emit_ffn(s)
        emit_ffn(s)
        if stop == f"ffn{s}":
            done = True

    DMA_SEMS = ["dini", "dinig", "dcs", "dwc0", "dwc1", "dwc2", "dws0", "dws1",
                "dws2", "dx0", "dx1", "dwv", "dwqk0",
                "dwqk1", "dsw", "dod", "dqm", "dpv", "dsh", "dxb", "dgu0",
                "dgu1", "dgu2", "ddw0", "ddw1", "ddw2",
                "dout0", "dout1", "ddbg", "dx", "dg"]
    P.op("sync", lambda e: None,
         waits=[(sm_, P.cnt.get(sm_, 0)) for sm_ in DMA_SEMS])

    from contextlib import ExitStack
    with ExitStack() as ctx:
        all_sems = ["pe", "sc", "ve", "gp", "dx", "dg", "dini", "dinig", "dcs",
                    "dwc0", "dwc1", "dwc2", "dws0", "dws1", "dws2",
                    "dx0", "dx1", "dx2", "dx3", "dwv",
                    "dwqk0", "dwqk1", "dsw", "dod", "dqm", "dpv", "dsh",
                    "dxb", "dgu0", "dgu1", "dgu2", "dgu3",
                    "ddw0", "ddw1", "ddw2", "ddw3", "dout0", "dout1", "ddbg"]
        semh = {nm: ctx.enter_context(nc.semaphore(nm)) for nm in all_sems}
        with nc.allow_low_precision(reason="bf16 matmul operand staging; "
                                     "fp32 accumulation kept in PSUM"), \
             nc.Block() as block:
            block.sync(lambda e: P.emit("sync", e, semh))
            block.gpsimd(lambda e: P.emit("gpsimd", e, semh))
            block.tensor(lambda e: P.emit("tensor", e, semh))
            block.scalar(lambda e: P.emit("scalar", e, semh))
            block.vector(lambda e: P.emit("vector", e, semh))
    return nc


def _rope_tables():
    dh = 64
    inv = 1.0 / (10000.0 ** (np.arange(0, dh, 2, dtype=np.float64) / dh))
    t = np.arange(S, dtype=np.float64)
    fr = np.outer(inv, t)
    emb = np.concatenate([fr, fr], axis=0)
    cos = np.cos(emb)
    sin = np.sin(emb)
    sgn = np.ones((dh, 1))
    sgn[: dh // 2] = -1.0
    sin = sin * sgn
    cos2 = np.concatenate([cos, cos], axis=0)
    sin2 = np.concatenate([sin, sin], axis=0)
    chi = cos2.astype(np.float32)
    shi = sin2.astype(np.float32)
    clo = (cos2 - chi.astype(np.float64)).astype(np.float32)
    slo = (sin2 - shi.astype(np.float64)).astype(np.float32)
    return (np.ascontiguousarray(chi), np.ascontiguousarray(shi),
            np.ascontiguousarray(clo), np.ascontiguousarray(slo))


_NC_CACHE = {}


def kernel(L0_attn_x, L0_attn_Wq, L0_attn_Wk, L0_attn_Wv, L0_attn_Wo,
           L0_ffn_router_w, L0_ffn_router_bias, L0_ffn_gate_w, L0_ffn_up_w,
           L0_ffn_down_w, L0_ffn_sgate_w, L0_ffn_sup_w, L0_ffn_sdown_w):
    x = np.asarray(L0_attn_x)
    cos2 = _rope_tables()
    com = {
        "wq_t": np.ascontiguousarray(np.asarray(L0_attn_Wq).T),
        "wk_t": np.ascontiguousarray(np.asarray(L0_attn_Wk).T),
        "wv_t": np.ascontiguousarray(np.asarray(L0_attn_Wv).T),
        "wo_t": np.ascontiguousarray(np.asarray(L0_attn_Wo).T),
        "rw_t": np.ascontiguousarray(np.asarray(L0_ffn_router_w).T),
        "rbias": np.ascontiguousarray(np.asarray(L0_ffn_router_bias).reshape(NE, 1)),
        "gw_t": np.ascontiguousarray(np.asarray(L0_ffn_gate_w).transpose(0, 2, 1)),
        "uw_t": np.ascontiguousarray(np.asarray(L0_ffn_up_w).transpose(0, 2, 1)),
        "dw_t": np.ascontiguousarray(np.asarray(L0_ffn_down_w).transpose(0, 2, 1)),
        "sg_t": np.ascontiguousarray(np.asarray(L0_ffn_sgate_w).T),
        "su_t": np.ascontiguousarray(np.asarray(L0_ffn_sup_w).T),
        "sd_t": np.ascontiguousarray(np.asarray(L0_ffn_sdown_w).T),
        "cos_t": cos2[0],
        "sin_t": cos2[1],
        "cos_lo": cos2[2],
        "sin_lo": cos2[3],
    }
    in_maps = []
    for c in range(NCORES):
        m = dict(com)
        m["x"] = np.ascontiguousarray(x[SEQ * c: SEQ * (c + 1)])
        in_maps.append(m)

    if "nc" not in _NC_CACHE:
        _NC_CACHE["nc"] = build_nc()
    nc = _NC_CACHE["nc"]
    res = run_bass_kernel_spmd(nc, in_maps, core_ids=list(range(NCORES)))
    outs = []
    for c in range(NCORES):
        yc = res.results[c]["y"]
        outs.append(np.ascontiguousarray(np.asarray(yc).T).reshape(SEQ, S, D))
    return np.concatenate(outs, axis=0)


# version 19 (final)
# speedup vs baseline: 1.0404x; 1.0404x over previous
import numpy as np
import ml_dtypes
import concourse.bass as bass
import concourse.mybir as mybir
from concourse.bass_utils import run_bass_kernel_spmd

F32 = mybir.dt.float32
BF16 = mybir.dt.bfloat16
AX = mybir.AxisListType.X
OP = mybir.AluOpType
AF = mybir.ActivationFunctionType

S, D = 2048, 1024
NCORES = 8
SEQ = 2
NE = 32
EPS = 1e-5

DBG = {"stop": None}

ENGINES = ("sync", "gpsimd", "tensor", "scalar", "vector")
COMPUTE = ("tensor", "scalar", "vector")


class Plan:
    def __init__(self):
        self.ops = {e: [] for e in ENGINES}
        self.cnt = {}
        self.events = {}

    def op(self, eng, fn, waits=(), inc=None, inc_n=1, key=None):
        if inc is not None:
            c = self.cnt.get(inc, 0) + inc_n
            self.cnt[inc] = c
            if key is not None:
                assert key not in self.events, f"dup event {key}"
                self.events[key] = (inc, c)
        self.ops[eng].append((fn, tuple(waits), inc, inc_n))

    def alias(self, key, src):
        self.events[key] = self.events[src]

    def marker(self, key, sem):
        assert key not in self.events, key
        self.events[key] = (sem, self.cnt.get(sem, 0))

    def fence(self, engs=COMPUTE, dma=True):
        sems = {"tensor": "pe", "scalar": "sc", "vector": "ve"}
        cur = {e: ("pe" if e == "tensor" else ("sc" if e == "scalar" else "ve"),
                   self.cnt.get(sems[e], 0)) for e in engs}
        if dma is True:
            dmaw = [(sm_, c) for sm_, c in self.cnt.items()
                    if not sm_.startswith(("_", "dwc", "dws")) and sm_ not in
                    ("pe", "sc", "ve", "gp")]
        elif dma:
            dmaw = [(sm_, c) for sm_, c in self.cnt.items() if sm_ in dma]
        else:
            dmaw = []
        for e in engs:
            waits = [cur[o] for o in engs if o != e] + dmaw
            self.op(e, lambda _: None, waits=waits)

    def emit(self, eng_name, eng, semh):
        waited = {}
        for fn, waits, inc, inc_n in self.ops[eng_name]:
            need = {}
            for w in waits:
                if isinstance(w, str):
                    s, v = self.events[w]
                else:
                    s, v = w
                if v > need.get(s, 0):
                    need[s] = v
            for s, v in need.items():
                if waited.get(s, -1) < v:
                    eng.wait_ge(semh[s], v)
                    waited[s] = v
            instr = fn(eng)
            if inc is not None:
                if instr is None:
                    eng.nop().then_inc(semh[inc], inc_n)
                else:
                    instr.then_inc(semh[inc], inc_n)


class SbufMap:
    def __init__(self, nc, base, top):
        self.nc, self.base, self.top, self.cur = nc, base, top, base
        self.marks = {}

    def alloc(self, name, shape, dtype, align=64):
        nb = int(np.prod(shape[1:])) * (2 if dtype == BF16 else 4)
        off = (self.cur + align - 1) // align * align
        t = self.nc.alloc_sbuf_tensor_at(name, list(shape), dtype, offset=off)
        self.cur = off + nb
        assert self.cur <= self.top, f"SBUF overflow at {name}: {self.cur}>{self.top}"
        return t

    def mark(self, n):
        self.marks[n] = self.cur

    def reset(self, n):
        self.cur = self.marks[n]


def build_nc():
    nc = bass.Bass("TRN2", target_bir_lowering=False)

    xin = nc.dram_tensor("x", [SEQ, S, D], F32, kind="ExternalInput").ap()
    wq = nc.dram_tensor("wq_t", [D, D], F32, kind="ExternalInput").ap()
    wk = nc.dram_tensor("wk_t", [D, D], F32, kind="ExternalInput").ap()
    wv = nc.dram_tensor("wv_t", [D, D], F32, kind="ExternalInput").ap()
    wo = nc.dram_tensor("wo_t", [D, D], F32, kind="ExternalInput").ap()
    rw = nc.dram_tensor("rw_t", [D, NE], F32, kind="ExternalInput").ap()
    rb = nc.dram_tensor("rbias", [NE, 1], F32, kind="ExternalInput").ap()
    gw = nc.dram_tensor("gw_t", [NE, D, 128], F32, kind="ExternalInput").ap()
    uw = nc.dram_tensor("uw_t", [NE, D, 128], F32, kind="ExternalInput").ap()
    dw = nc.dram_tensor("dw_t", [NE, 128, D], F32, kind="ExternalInput").ap()
    sg = nc.dram_tensor("sg_t", [D, 512], F32, kind="ExternalInput").ap()
    su = nc.dram_tensor("su_t", [D, 512], F32, kind="ExternalInput").ap()
    sd = nc.dram_tensor("sd_t", [512, D], F32, kind="ExternalInput").ap()
    ctab = nc.dram_tensor("cos_t", [128, S], F32, kind="ExternalInput").ap()
    stab = nc.dram_tensor("sin_t", [128, S], F32, kind="ExternalInput").ap()
    ctlo = nc.dram_tensor("cos_lo", [128, S], F32, kind="ExternalInput").ap()
    stlo = nc.dram_tensor("sin_lo", [128, S], F32, kind="ExternalInput").ap()
    y = nc.dram_tensor("y", [D, SEQ * S], F32, kind="ExternalOutput").ap()
    gwc = nc.dram_tensor("gw_bf", [NE, 128, 1024], BF16, kind="Internal").ap()
    uwc = nc.dram_tensor("uw_bf", [NE, 128, 1024], BF16, kind="Internal").ap()
    dwc = nc.dram_tensor("dw_bf", [NE, 128, 1024], BF16, kind="Internal").ap()

    P = Plan()

    base = ((nc.sbuf_base + 63) // 64) * 64
    sm = SbufMap(nc, base, nc.sbuf_top)

    xn = sm.alloc("xn", [128, 8 * 2048], BF16)
    pv = sm.alloc("pv", [128, 8 * 2048], BF16)
    cosb = sm.alloc("cosb", [128, S], BF16)
    sinb = sm.alloc("sinb", [128, S], BF16)
    ident = sm.alloc("ident", [128, 128], BF16)
    trim = sm.alloc("trim", [128, 128], BF16)
    onesb = sm.alloc("onesb", [128, 128], BF16)
    ones2h = sm.alloc("ones2h", [128, 2], BF16)
    rwsb = sm.alloc("rwsb", [128, 8 * NE], BF16)
    stage = [sm.alloc(f"stage{i}", [128, 1024], BF16) for i in range(3)]
    rbsb = sm.alloc("rbsb", [NE, 1], F32)
    k2m = sm.alloc("k2m", [2, 8], F32)
    sm.mark("arena")

    x_sb = [sm.alloc(f"x_sb{i}", [128, D], BF16) for i in range(2)]
    xn_tm = [sm.alloc(f"xn_tm{i}", [128, D], BF16) for i in range(2)]
    bn6 = sm.alloc("bn6", [128, 16], F32)
    mv = sm.alloc("mv", [128, 8], F32)
    v_sb = sm.alloc("v_sb", [128, 16 * 1040], BF16)
    p_T = sm.alloc("p_T", [128, 16 * 512], BF16)
    q_ext = [sm.alloc(f"q_ext{i}", [65, S], BF16) for i in range(4)]
    k_ext = [sm.alloc(f"k_ext{i}", [65, S], BF16) for i in range(4)]
    raw = sm.alloc("raw", [128, S], BF16)
    swp = sm.alloc("swp", [128, S], BF16)
    t1b = sm.alloc("t1b", [128, S], BF16)
    sqb = t1b
    mrow = sm.alloc("mrow", [2, S], BF16)
    k2t = sm.alloc("k2t", [2, 8], F32)
    lrec = sm.alloc("lrec", [128, 512], BF16)
    lbc_sb = [sm.alloc(f"lbc{i}", [128, 512], BF16) for i in range(2)]
    pvn = sm.alloc("pvn", [65, S], BF16)
    wqk_sb = [sm.alloc(f"wqk{i}", [128, 2 * 8 * 128], BF16) for i in range(2)]
    wv_sb = sm.alloc("wv_sb", [128, 8 * 512], BF16)
    sm.reset("arena")

    h_sb = sm.alloc("h_sb", [128, NE * 512], BF16)
    sh_sb = sm.alloc("sh_sb", [128, 4 * 512], BF16)
    x_bf = [sm.alloc("x_bf0", [128, 4 * 1024], BF16)] * 2
    gu_sb = [sm.alloc(f"gu_sb{i}", [128, 2 * 1024], BF16) for i in range(3)]
    dw_sb = [sm.alloc(f"dw_sb{i}", [128, 1024], BF16) for i in range(3)]
    sgs = sm.alloc("sgs", [128, 8 * 512], BF16)
    sus = sm.alloc("sus", [128, 8 * 512], BF16)
    sds = sm.alloc("sds", [128, 4 * 1024], BF16)
    wos = sm.alloc("wos", [128, 8 * 1024], BF16)
    wbc_sb = [sm.alloc(f"wbc{i}", [128, 512], BF16) for i in range(2)]
    gs_sb = sm.alloc("gs_sb", [128, 512], BF16)
    tt_sb = sm.alloc("tt_sb", [128, 512], BF16)
    lem = sm.alloc("lem", [NE, S], BF16)
    wtm = sm.alloc("wtm", [128, 16 * NE], BF16)
    ssf = sm.alloc("ssf", [128, NE], F32)
    t8f = sm.alloc("t8f", [128, 8], F32)
    mskf = sm.alloc("mskf", [128, NE], F32)
    nmf = sm.alloc("nmf", [128, 8], F32)
    out_sb = [sm.alloc(f"out_sb{i}", [128, 512], F32) for i in range(2)]
    selw = sm.alloc("selw", [NE, NE * 128], BF16)
    wem = sm.alloc("wem", [NE, S], BF16)

    ps4 = nc.alloc_psum_tensor("ps4", [128, 2048], F32)
    pvb = [nc.alloc_psum_tensor(f"pvb{i}", [128, 512], F32) for i in range(2)]
    pq2 = nc.alloc_psum_tensor("pq2", [128, 512], F32)
    plb = nc.alloc_psum_tensor("plb", [128, 512], F32)

    def s4(i):
        return ps4[:, (i % 4) * 512:(i % 4 + 1) * 512]

    WC = {"idx": 0, "jobs": []}

    def allw():
        return [(sm_, c) for sm_, c in P.cnt.items()
                if not sm_.startswith(("_", "dwc", "dws"))]

    def dma(engq, out, in_, waits=(), key=None, sem=None):
        sem = sem or ("dg" if engq == "gpsimd" else "dx")
        P.op(engq, lambda e, o=out, i=in_: e.dma_start(out=o, in_=i),
             waits=waits, inc=sem, inc_n=16, key=key)

    def emit_init():
        def g_init(e):
            if DBG.get("stop"):
                for t in (v_sb, pv, p_T, xn):
                    e.memset(t[:, :], 0.0)
                e.drain()
            e.memset(ident[:, :], 1.0)
            e.memset(trim[:, :], 1.0)
            e.memset(ones2h[:, :], 1.0)
            e.memset(onesb[:, :], 1.0)
            e.drain()
            e.affine_select(out=ident[:, :], in_=ident[:, :],
                            compare_op=OP.is_equal, fill=0.0, base=0,
                            pattern=[[-1, 128]], channel_multiplier=1)
            e.affine_select(out=trim[:, :], in_=trim[:, :],
                            compare_op=OP.is_ge, fill=0.0, base=0,
                            pattern=[[1, 128]], channel_multiplier=-1)
            e.affine_select(out=ones2h[:, :], in_=ones2h[:, :],
                            compare_op=OP.is_ge, fill=0.0, base=0,
                            pattern=[[-64, 2]], channel_multiplier=1)
            e.drain()
            r = e.affine_select(out=ones2h[:, :], in_=ones2h[:, :],
                                compare_op=OP.is_ge, fill=0.0, base=63,
                                pattern=[[64, 2]], channel_multiplier=-1)
            return r
        P.op("gpsimd", g_init, inc="gp", inc_n=1, key="init_gp")
        dma("gpsimd", cosb[:, :], ctab, key="ld_cos", sem="dcs")
        dma("gpsimd", sinb[:, :], stab, key="ld_sin", sem="dcs")
        dma("gpsimd", raw[:, :], ctlo, key="ld_clo", sem="dcs")
        dma("gpsimd", swp[:, :], stlo, key="ld_slo", sem="dcs")
        P.op("vector", lambda e: e.tensor_add(cosb[:, :], cosb[:, :], raw[:, :]),
             waits=["ld_slo"], inc="ve", inc_n=1, key="cs_add0")
        P.op("vector", lambda e: e.tensor_add(sinb[:, :], sinb[:, :], swp[:, :]),
             waits=["ld_slo"], inc="ve", inc_n=1, key="cs_add1")
        dma("sync", rbsb[:, :], rb, key="ld_rb", sem="dini")
        dma("gpsimd", rwsb[:, :].rearrange("p (k e) -> p k e", k=8),
            rw.rearrange("(k p) e -> p k e", p=128), key="ld_rw", sem="dinig")
        for ei in range(NE):
            for (src, dst) in ((gw, gwc), (uw, uwc), (dw, dwc)):
                WC["jobs"].append((src, dst, ei))

    def drip_wc(n):
        for _ in range(n):
            idx = WC["idx"]
            if idx >= len(WC["jobs"]):
                return
            WC["idx"] = idx + 1
            src, dst, ei = WC["jobs"][idx]
            sb = stage[idx % 3]
            w = [f"wc_st_{idx - 3}"] if idx >= 3 else []
            if dst is dwc:
                in_ap = src[ei]
                out_ap = sb[:, :]
            else:
                in_ap = src[ei].rearrange("(k p) h -> p k h", p=128)
                out_ap = sb[:, :].rearrange("p (k h) -> p k h", k=8)
            P.op("gpsimd", lambda e, o=out_ap, i=in_ap: e.dma_start(out=o, in_=i),
                 waits=w, inc=f"dwc{idx % 3}", inc_n=16, key=f"wc_ld_{idx}")
            P.op("sync", lambda e, sb=sb, d=dst[ei]: e.dma_start(out=d, in_=sb[:, :]),
                 waits=[f"wc_ld_{idx}"], inc=f"dws{idx % 3}", inc_n=16,
                 key=f"wc_st_{idx}")

    def emit_ln(s):
        for i in range(16):
            xw = [f"ln_ve_{s}_{i - 2}"] if i >= 2 else (allw() if s > 0 else [])
            dma("gpsimd", x_sb[i % 2][:, :], xin[s, i * 128:(i + 1) * 128, :],
                waits=xw, key=f"ln_x_{s}_{i}", sem=f"dx{i % 2}")
            if s == 0:
                drip_wc(1)

            def stats(e, i=i):
                xs = x_sb[i % 2]
                e.bn_stats(bn6[:, 0:6], xs[:, 0:512])
                e.bn_stats(bn6[:, 6:12], xs[:, 512:1024])
                means = bass.AP(bn6, 1, [[16, 128], [3, 4]])
                m2s = bass.AP(bn6, 2, [[16, 128], [3, 4]])
                def nops(n=4):
                    e.drain()
                nops()
                e.reduce_sum(mv[:, 0:1], means, axis=AX)
                e.reduce_sum(mv[:, 1:2], m2s, axis=AX)
                e.scalar_tensor_tensor(bn6[:, 12:16], means, 1.0, means,
                                       op0=OP.mult, op1=OP.mult)
                nops()
                e.reduce_sum(mv[:, 2:3], bn6[:, 12:16], axis=AX)
                e.tensor_scalar(mv[:, 3:4], mv[:, 1:2], 1.0 / D, None,
                                op0=OP.mult)
                nops()
                e.tensor_scalar(mv[:, 4:5], mv[:, 2:3], 0.25, mv[:, 3:4],
                                op0=OP.mult, op1=OP.add)
                e.scalar_tensor_tensor(mv[:, 5:6], mv[:, 0:1], -1.0 / 16.0,
                                       mv[:, 0:1], op0=OP.mult, op1=OP.mult)
                nops()
                e.tensor_scalar(mv[:, 4:5], mv[:, 4:5], mv[:, 5:6], EPS,
                                op0=OP.add, op1=OP.add)
                e.tensor_scalar(mv[:, 0:1], mv[:, 0:1], 0.25, None,
                                op0=OP.mult)
                nops()
                return e.reciprocal(mv[:, 5:6], mv[:, 4:5])
            sw = [f"ln_x_{s}_{i}"] + ([f"ln_sq_{s}_{i - 1}"] if i >= 1 else [])
            P.op("vector", stats, waits=sw, inc="ve", inc_n=1, key=f"ln_st_{s}_{i}")
            P.op("scalar", lambda e: e.activation(mv[:, 6:7], mv[:, 5:6], AF.Sqrt),
                 waits=[f"ln_st_{s}_{i}"], inc="sc", inc_n=1, key=f"ln_sq_{s}_{i}")
            vw = [f"ln_sq_{s}_{i}"] + ([f"ln_tr_{s}_{i - 2}"] if i >= 2 else [])
            P.op("vector", lambda e, i=i: e.tensor_scalar(
                xn_tm[i % 2][:, :], x_sb[i % 2][:, :], mv[:, 0:1], mv[:, 6:7],
                op0=OP.subtract, op1=OP.mult),
                waits=vw, inc="ve", inc_n=1, key=f"ln_ve_{s}_{i}")

            def trans(e, i=i):
                r = None
                for dc in range(8):
                    r = e.matmul(ps4[:, (dc // 4) * 512 + (dc % 4) * 128:
                                      (dc // 4) * 512 + (dc % 4 + 1) * 128],
                                 xn_tm[i % 2][:, dc * 128:(dc + 1) * 128],
                                 ident[:, :], start=True, stop=True)
                return r
            tw = [f"ln_ve_{s}_{i}", "init_gp"] + \
                 ([f"ln_ev_{s}_{i - 1}_1"] if i >= 1 else [])
            P.op("tensor", trans, waits=tw, inc="pe", inc_n=1, key=f"ln_tr_{s}_{i}")
            for b in range(2):
                def evac(e, i=i, b=b):
                    o = bass.AP(xn, (4 * b) * 2048 + i * 128,
                                [[8 * 2048, 128], [2048, 4], [1, 128]])
                    return e.copy(o, bass.AP(ps4, b * 512, [[2048, 128], [128, 4], [1, 128]]))
                P.op("scalar", evac, waits=[f"ln_tr_{s}_{i}"],
                     inc="sc", inc_n=1, key=f"ln_ev_{s}_{i}_{b}")

    def emit_attn_group(s, g):
        dcp_lim = DBG.get("dcp_lim", 4)
        head_lim = DBG.get("head_lim", 2)
        v_lim = DBG.get("v_lim", 16)
        if g == 0:
            def ainit(e):
                e.memset(bass.AP(v_sb, 64,
                                 [[16 * 1040, 128], [1040, 16], [65, 16], [1, 1]]), 1.0)
                r = None
                for i in range(4):
                    r = e.memset(k_ext[i][64:65, :], -1.0)
                return r
            P.op("gpsimd", ainit, waits=allw(), inc="gp", inc_n=1,
                 key=f"ainit_{s}")
        def mk_vproj_units(g=g):
            vslice = wv.rearrange("(k p) o -> p k o",
                                  p=128)[:, :, g * 512:(g + 1) * 512]
            vw_w = [f"v_pe_done_{s}_{g - 1}"] if g > 0 else allw()
            P.op("gpsimd", lambda e: e.dma_start(
                out=wv_sb[:, :].rearrange("p (k o) -> p k o", k=8), in_=vslice),
                waits=vw_w, inc="dwv", inc_n=16, key=f"wv_ld_{s}_{g}")
            units = []
            for i in range(v_lim):
                def unit(i=i, g=g):
                    def vproj(e, i=i):
                        r = None
                        for kc in range(8):
                            r = e.matmul(pvb[1][:, :],
                                         xn[:, kc * 2048 + i * 128: kc * 2048 + (i + 1) * 128],
                                         wv_sb[:, kc * 512:(kc + 1) * 512],
                                         start=(kc == 0), stop=(kc == 7))
                        return r
                    vw = [f"wv_ld_{s}_{g}"]
                    if i >= 1:
                        vw += [f"v_ev_{s}_{g}_{i - 1}"]
                    elif g > 0:
                        vw += [f"q_ev_{s}_{g - 1}_3_3"]
                    P.op("tensor", vproj, waits=vw, inc="pe", inc_n=1,
                         key=f"v_pe_{s}_{g}_{i}")

                    def vevac(e, i=i, g=g):
                        o = bass.AP(v_sb, i * 1040 + (g * 8) * 65,
                                    [[16 * 1040, 128], [65, 8], [1, 64]])
                        ip = bass.AP(pvb[1], 0, [[512, 128], [64, 8], [1, 64]])
                        return e.tensor_copy(o, ip)
                    P.op("vector", vevac,
                         waits=[f"v_pe_{s}_{g}_{i}", f"ainit_{s}"],
                         inc="ve", inc_n=1, key=f"v_ev_{s}_{g}_{i}")
                    if s == 0:
                        drip_wc(1)
                units.append(unit)
            return units
        if g == 0:
            for u in mk_vproj_units():
                u()
        P.marker(f"v_pe_done_{s}_{g}", "pe")

        def prev_key(dcp, fmt, back=1):
            ddc = dcp - back
            ss, gg = s, g
            while ddc < 0:
                gg -= 1
                if gg < 0:
                    ss -= 1
                    gg = 1
                if ss < 0:
                    return None
                ddc += 4
            kk = fmt.format(s=ss, g=gg, dcp=ddc)
            return kk if kk in P.events else None

        def emit_prep(dcp):
            dc = 4 * g + dcp
            key = f"{s}_{g}_{dcp}"
            qeA, qeB = q_ext[(2 * dcp) % 4], q_ext[(2 * dcp + 1) % 4]
            keA, keB = k_ext[(2 * dcp) % 4], k_ext[(2 * dcp + 1) % 4]
            wqk = wqk_sb[dcp % 2]
            kslice = wk.rearrange("(k p) o -> p k o", p=128)[:, :, dc * 128:(dc + 1) * 128]
            qslice = wq.rearrange("(k p) o -> p k o", p=128)[:, :, dc * 128:(dc + 1) * 128]
            if dcp >= 2:
                ww = [f"q_pe_{s}_{g}_{dcp - 2}_3"]
            elif g > 0:
                ww = [f"q_pe_{s}_{g - 1}_{dcp + 2}_3"]
            else:
                ww = allw()
            P.op("gpsimd", lambda e, ks=kslice, wqk=wqk: e.dma_start(
                out=wqk[:, 0:1024].rearrange("p (k o) -> p k o", k=8), in_=ks),
                waits=ww, inc=f"dwqk{dcp % 2}", inc_n=16, key=f"wk_ld_{key}")
            P.op("gpsimd", lambda e, qs=qslice, wqk=wqk: e.dma_start(
                out=wqk[:, 1024:2048].rearrange("p (k o) -> p k o", k=8), in_=qs),
                waits=ww, inc=f"dwqk{dcp % 2}", inc_n=16, key=f"wq_ld_{key}")
            if s == 0:
                drip_wc(2)

            chunks = []
            for which in ("k", "q"):
                woff = 0 if which == "k" else 1024
                def norm_mm(nt, which=which):
                    if which == "k":
                        w = [f"k_sq_{key}_{nt}"] +                             ([f"k2r_{key}_{nt - 1}"] if nt else
                             ([prev_key(dcp, "mrow_{s}_{g}_{dcp}_3")] if
                              prev_key(dcp, "mrow_{s}_{g}_{dcp}_3") else []))
                        P.op("tensor", lambda e, nt=nt: e.matmul(
                            pq2[0:2, :], ones2h[:, :],
                            sqb[:, nt * 512:(nt + 1) * 512],
                            start=True, stop=True),
                            waits=w, inc="pe", inc_n=1, key=f"k2_pe_{key}_{nt}")
                        def k2red(e, nt=nt):
                            r = e.reduce_max(k2t[0:2, nt:nt + 1], pq2[0:2, :], axis=AX)
                            if nt == 3:
                                e.drain()
                                e.reduce_max(k2m[0:2, 0:1], k2t[0:2, 0:4], axis=AX)
                                e.drain()
                                r = e.tensor_scalar(k2m[0:2, 1:2], k2m[0:2, 0:1],
                                                    0.5, None, op0=OP.mult)
                            return r
                        P.op("vector", k2red, waits=[f"k2_pe_{key}_{nt}"],
                             inc="ve", inc_n=1, key=f"k2r_{key}_{nt}")
                    else:
                        mg = [prev_key(dcp, "qm_{s}_{g}_{dcp}_1")]                             if prev_key(dcp, "qm_{s}_{g}_{dcp}_1") else []
                        P.op("tensor", lambda e, nt=nt: e.matmul(
                            pq2[0:2, :], ones2h[:, :],
                            sqb[:, nt * 512:(nt + 1) * 512],
                            start=True, stop=True),
                            waits=[f"q_sq_{key}_{nt}", f"k2r_{key}_3"] +
                                  ([f"mrow_{key}_{nt - 1}"] if nt else []),
                            inc="pe", inc_n=1, key=f"q2_pe_{key}_{nt}")
                        P.op("scalar", lambda e, nt=nt: e.activation(
                            mrow[0:2, nt * 512:(nt + 1) * 512], pq2[0:2, :],
                            AF.Relu, bias=k2m[0:2, 1:2], scale=0.5),
                            waits=[f"q2_pe_{key}_{nt}"] + (mg if nt == 0 else []),
                            inc="sc", inc_n=1, key=f"mrow_{key}_{nt}")
                def proj_chunk(nt, which=which, woff=woff, wqk=wqk,
                               norm_mm=norm_mm):
                    def proj(e, nt=nt, woff=woff, wqk=wqk):
                        r = None
                        for kc in range(8):
                            r = e.matmul(pvb[1][:, :],
                                         wqk[:, woff + kc * 128: woff + (kc + 1) * 128],
                                         xn[:, kc * 2048 + nt * 512: kc * 2048 + (nt + 1) * 512],
                                         start=(kc == 0), stop=(kc == 7))
                        return r
                    pw = [f"wq_ld_{key}"]
                    if nt >= 1:
                        pw += [f"{which}_ev_{key}_{nt - 1}"]
                    elif which == "q":
                        pw += [f"k_ev_{key}_3"]
                    elif dcp == 0:
                        pw += [f"v_ev_{s}_{g}_15"]
                    else:
                        pkv = prev_key(dcp, "q_ev_{s}_{g}_{dcp}_3")
                        if pkv:
                            pw += [pkv]
                    if nt == 0:
                        if which == "q":
                            pw += [f"k_t1_{key}", f"k_t2_{key}", f"k2_pe_{key}_3",
                                   f"ksw_{key}_3"]
                        else:
                            for fmt in ("q_t1_{s}_{g}_{dcp}", "q_t2_{s}_{g}_{dcp}",
                                        "q2_pe_{s}_{g}_{dcp}_3", "qsw_{s}_{g}_{dcp}_3"):
                                pk = prev_key(dcp, fmt)
                                if pk:
                                    pw += [pk]
                    P.op("tensor", proj, waits=pw, inc="pe", inc_n=1,
                         key=f"{which}_pe_{key}_{nt}")
                    P.op("vector", lambda e, nt=nt: e.tensor_copy(
                        raw[:, nt * 512:(nt + 1) * 512], pvb[1][:, :]),
                        waits=[f"{which}_pe_{key}_{nt}"],
                        inc="ve", inc_n=1, key=f"{which}_ev_{key}_{nt}")
                    sqg = []
                    if nt == 0:
                        if which == "q":
                            sqg = [f"k_e3a_{key}", f"k_e3b_{key}"]
                        else:
                            for fmt in ("q_e3a_{s}_{g}_{dcp}", "q_e3b_{s}_{g}_{dcp}"):
                                pk = prev_key(dcp, fmt)
                                if pk:
                                    sqg += [pk]
                    P.op("vector", lambda e, nt=nt: e.tensor_mul(
                        sqb[:, nt * 512:(nt + 1) * 512],
                        raw[:, nt * 512:(nt + 1) * 512],
                        raw[:, nt * 512:(nt + 1) * 512]),
                        waits=[f"{which}_ev_{key}_{nt}"] + sqg,
                        inc="ve", inc_n=1, key=f"{which}_sq_{key}_{nt}")
                    if nt >= 1:
                        norm_mm(nt - 1)
                for nt in range(4):
                    chunks.append(lambda nt=nt, pc=proj_chunk: pc(nt))

                def rope_a(which=which, norm_mm=norm_mm, wqk=wqk):
                    norm_mm(3)
                    evd = f"{which}_ev_{key}_3"
                    swg = []
                    if which == "q":
                        swg = [f"k_e3a_{key}", f"k_e3b_{key}", f"k_odd_{key}"]
                    else:
                        for fmt in ("q_e3a_{s}_{g}_{dcp}", "q_e3b_{s}_{g}_{dcp}",
                                    "q_odd_{s}_{g}_{dcp}"):
                            pk = prev_key(dcp, fmt)
                            if pk:
                                swg.append(pk)
                    for bi, (d0, s0) in enumerate(((0, 32), (32, 0), (64, 96),
                                                   (96, 64))):
                        dma("sync", swp[d0:d0 + 32, :], raw[s0:s0 + 32, :],
                            waits=[evd] + swg, key=f"{which}sw_{key}_{bi}",
                            sem="dsw")
                    swd = [f"{which}sw_{key}_{bi}" for bi in range(4)]
                    P.op("vector", lambda e: e.tensor_mul(t1b[:, :], raw[:, :],
                                                          cosb[:, :]),
                         waits=[evd, "cs_add1",
                                f"{'k2' if which == 'k' else 'q2'}_pe_{key}_3"],
                         inc="ve", inc_n=1, key=f"{which}_t1_{key}")
                    P.op("vector", lambda e: e.tensor_mul(swp[:, :], swp[:, :],
                                                          sinb[:, :]),
                         waits=swd + ["cs_add1"], inc="ve", inc_n=1,
                         key=f"{which}_t2_{key}")

                def rope_b(which=which):
                    eA, eB = (keA, keB) if which == "k" else (qeA, qeB)
                    extw = [f"{which}_t1_{key}", f"{which}_t2_{key}"]
                    pk = prev_key(dcp, "sdpaA_{s}_{g}_{dcp}", back=2)
                    if pk:
                        extw += [pk, prev_key(dcp, "sdpaB_{s}_{g}_{dcp}", back=2)]
                    P.op("vector", lambda e, eA=eA: e.tensor_add(
                        eA[0:64, :], t1b[0:64, :], swp[0:64, :]),
                        waits=extw, inc="ve", inc_n=1, key=f"{which}_e3a_{key}")
                    odg = []
                    if which == "q":
                        odg = [f"k_odd_{key}"]
                    else:
                        pk = prev_key(dcp, "q_odd_{s}_{g}_{dcp}")
                        odg = [pk] if pk else []
                    P.op("vector", lambda e: e.tensor_add(
                        swp[64:128, :], t1b[64:128, :], swp[64:128, :]),
                        waits=extw + odg, inc="ve", inc_n=1,
                        key=f"{which}_e3b_{key}")
                    dma("sync", eB[0:64, :], swp[64:128, :],
                        waits=[f"{which}_e3b_{key}"], key=f"{which}_odd_{key}",
                        sem="dod")
                    if which == "q":
                        for hh, qe in ((0, qeA), (1, qeB)):
                            dma("sync", qe[64:65, :], mrow[hh:hh + 1, :],
                                waits=[f"mrow_{key}_3"], key=f"qm_{key}_{hh}",
                                sem="dqm")
                chunks.append(rope_a)
                chunks.append(rope_b)
            return chunks

        fill_state = {"q": [], "slot": 0, "pulled": 0, "npc": 12}

        def pull():
            fill_state["slot"] += 1
            while (fill_state["q"] and
                   fill_state["pulled"] * 16 < fill_state["slot"] *
                   fill_state["npc"]):
                fill_state["q"].pop(0)()
                fill_state["pulled"] += 1

        def emit_sdpa_pair(dcp):
            dc = 4 * g + dcp
            key = f"{s}_{g}_{dcp}"
            qeA, qeB = q_ext[(2 * dcp) % 4], q_ext[(2 * dcp + 1) % 4]
            keA, keB = k_ext[(2 * dcp) % 4], k_ext[(2 * dcp + 1) % 4]
            pending_lbc = []

            def flush_lbc():
                while pending_lbc:
                    fn, waits, keyx = pending_lbc.pop(0)
                    P.op("tensor", fn, waits=waits, inc="pe", inc_n=1, key=keyx)

            pending_vec = []

            def flush_vec():
                while pending_vec:
                    fn, waits, keyx = pending_vec.pop(0)
                    P.op("vector", fn, waits=waits, inc="ve", inc_n=1, key=keyx)

            pending_dma = []

            def flush_dma():
                while pending_dma:
                    out_, in_, waits, keyx = pending_dma.pop(0)
                    dma("sync", out_, in_, waits=waits, key=keyx, sem="dpv")

            chunks = []
            for hh in range(head_lim):
                h = 2 * dc + hh
                qe, ke = (qeA, keA) if hh == 0 else (qeB, keB)
                ready = [f"v_pe_done_{s}_{g}", "init_gp", f"ainit_{s}",
                         f"v_ev_{s}_{g}_15"]
                if hh == 0:
                    ready += [f"q_e3a_{key}", f"k_e3a_{key}", f"qm_{key}_1"]
                else:
                    ready += [f"q_odd_{key}", f"k_odd_{key}", f"qm_{key}_1"]

                def it_chunk(it, hh=hh, h=h, qe=qe, ke=ke, ready=ready):
                    if s == 0 and it == 0:
                        drip_wc(2)
                    qk_idx = 2 * it * (it + 1)
                    jmax = 4 * (it + 1)
                    tiles = []
                    for jc in range(jmax):
                        tiles.append((jc, qk_idx % 4, qk_idx))
                        qk_idx += 1
                    runs = []
                    cur = []
                    for (jc, slot, qi) in tiles:
                        if cur and (slot == 0 or len(cur) == 4):
                            runs.append(cur)
                            cur = []
                        cur.append((jc, slot, qi))
                    if cur:
                        runs.append(cur)
                    first_run = True
                    for run in runs:
                        for (jc, slot, qi) in run:
                            w = list(ready) if qi == 0 else []
                            if qi >= 4:
                                w += [f"expb_{s}_{h}_{qi - 4}"]
                            else:
                                if h > 0:
                                    w += [f"sdpa_expdone_{s}_{h - 1}"]
                                if dcp == 0 and hh == 0 and g == 0 and qi < 2:
                                    w += [f"ln_ev_{s}_15_{qi}"]
                            cq = 0
                            P.op("tensor", lambda e, jc=jc, slot=slot, it=it,
                                 qe=qe, ke=ke, cq=cq: e.matmul(
                                     ps4[:, slot * 512 + cq:(slot + 1) * 512],
                                     ke[0:65, jc * 128:(jc + 1) * 128],
                                     qe[0:65, it * 512 + cq:(it + 1) * 512],
                                     start=True, stop=True),
                                 waits=w, inc="pe", inc_n=1, key=f"qk_{s}_{h}_{qi}")
                        if first_run:
                            flush_lbc()
                            flush_vec()
                            flush_dma()
                            first_run = False
                        jc0, slot0, qi0 = run[0]
                        qiL = run[-1][2]
                        ew = [f"qk_{s}_{h}_{qiL}"]
                        if it > 0:
                            ew += [f"pv_{s}_{h}_{it - 1}"]
                        elif h > 0:
                            ew += [f"pv_{s}_{h - 1}_3"]
                        nrun = len(run)
                        P.op("scalar", lambda e, slot0=slot0, jc0=jc0, nrun=nrun:
                             e.activation(
                                 p_T[:, jc0 * 512: (jc0 + nrun) * 512],
                                 ps4[:, slot0 * 512: (slot0 + nrun) * 512],
                                 AF.Exp, scale=0.125),
                             waits=ew, inc="sc", inc_n=1, key=f"expb_{s}_{h}_{qi0}")
                        for (jc, slot, qi) in run[1:]:
                            P.alias(f"expb_{s}_{h}_{qi}", f"expb_{s}_{h}_{qi0}")
                        for (jc, slot, qi) in run:
                            if jc < 4 * it:
                                continue
                            c = 128 * jc - 512 * it
                            if c > 0:
                                P.op("vector", lambda e, jc=jc, c=c: e.memset(
                                    p_T[:, jc * 512: jc * 512 + c], 0.0),
                                    waits=[f"expb_{s}_{h}_{qi0}"],
                                    inc="ve", inc_n=1, key=f"pz_{s}_{h}_{qi}")
                            P.op("vector", lambda e, jc=jc, c=c: e.tensor_mul(
                                p_T[:, jc * 512 + c: jc * 512 + c + 128],
                                p_T[:, jc * 512 + c: jc * 512 + c + 128],
                                trim[:, :]),
                                waits=[f"expb_{s}_{h}_{qi0}"],
                                inc="ve", inc_n=1, key=f"msk_{s}_{h}_{qi}")
                        pull()
                    base_qi = qk_idx - jmax
                    for jc in range(jmax):
                        qi = base_qi + jc
                        c = max(0, 128 * jc - 512 * it)
                        w = [f"expb_{s}_{h}_{qi}"]
                        if jc >= 4 * it:
                            w += [f"msk_{s}_{h}_{qi}"]
                            if c > 0:
                                w += [f"pz_{s}_{h}_{qi}"]
                        if jc == 0:
                            if it >= 1:
                                w += [f"pvn_{s}_{h}_{it - 1}"]
                            elif h > 0:
                                w += [f"pvn_{s}_{h - 1}_3"]
                        P.op("tensor", lambda e, it=it, jc=jc, h=h, jmax=jmax: e.matmul(
                            pvb[0][0:65, :],
                            v_sb[:, jc * 1040 + h * 65: jc * 1040 + h * 65 + 65],
                            p_T[:, jc * 512: (jc + 1) * 512],
                            start=(jc == 0), stop=(jc == jmax - 1)),
                            waits=w, inc="pe", inc_n=1,
                            key=(f"pv_{s}_{h}_{it}" if jc == jmax - 1 else None))
                    lg = [f"pv_{s}_{h}_{it}"]
                    if it >= 1:
                        lg += [f"lbc_pe_{s}_{h}_{it - 1}"]
                    elif h > 0:
                        lg += [f"lbc_pe_{s}_{h - 1}_3"]
                    P.op("vector", lambda e, it=it: e.tensor_copy(
                        lrec[64:65, :], pvb[0][64:65, :]),
                        waits=lg, inc="ve", inc_n=1, key=f"lcp_{s}_{h}_{it}")
                    bw = [f"lcp_{s}_{h}_{it}"]
                    if it >= 1:
                        bw += [f"rcp_{s}_{h}_{it - 1}"]
                    elif h > 0:
                        bw += [f"rcp_{s}_{h - 1}_3"]
                    pending_lbc.append((
                        lambda e: e.matmul(plb[0:64, :], onesb[64:65, 0:64],
                                           lrec[64:65, :], start=True, stop=True),
                        bw, f"lbc_pe_{s}_{h}_{it}"))
                    rg = [f"lbc_pe_{s}_{h}_{it}"]
                    if it >= 2:
                        rg += [f"pvn_{s}_{h}_{it - 2}"]
                    elif h > 0:
                        rg += [f"pvn_{s}_{h - 1}_{it + 2}"]
                    pending_vec.append((
                        lambda e, it=it: e.reciprocal(
                            lbc_sb[it % 2][0:64, :], plb[0:64, :]),
                        rg, f"rcp_{s}_{h}_{it}"))
                    pg = [f"rcp_{s}_{h}_{it}"]
                    if it == 0 and h > 0:
                        pg += [f"pvdma_{s}_{h - 1}"]
                    pending_vec.append((
                        lambda e, it=it: e.tensor_mul(
                            pvn[0:64, it * 512:(it + 1) * 512],
                            pvb[0][0:64, :],
                            lbc_sb[it % 2][0:64, :]),
                        pg, f"pvn_{s}_{h}_{it}"))
                    if it == 3:
                        P.marker(f"sdpa_expdone_{s}_{h}", "sc")
                        pending_dma.append((
                            pv[(h % 2) * 64:(h % 2) * 64 + 64,
                               (h // 2) * 2048:(h // 2) * 2048 + 2048],
                            pvn[0:64, :],
                            [f"pvn_{s}_{h}_{i2}" for i2 in range(4)],
                            f"pvdma_{s}_{h}"))
                        P.marker(f"sdpaA_{key}" if hh == 0 else f"sdpaB_{key}",
                                 "pe")

                for it in range(4):
                    chunks.append(lambda it=it, ic=it_chunk: ic(it))
            chunks.append(lambda: (flush_lbc(), flush_vec(), flush_dma()))
            return chunks

        for step in range(dcp_lim + 1):
            if step < dcp_lim:
                pchunks = emit_prep(step)
            elif g == 0 and step == 4:
                pchunks = mk_vproj_units(g + 1)
            else:
                pchunks = []
            schunks = emit_sdpa_pair(step - 1) if step >= 1 else []
            if not schunks:
                for c in pchunks:
                    c()
                continue
            fill_state["q"] = list(pchunks)
            fill_state["slot"] = 0
            fill_state["pulled"] = 0
            fill_state["npc"] = max(len(pchunks), 1)
            for c in schunks:
                c()
            for c in fill_state["q"]:
                c()
            fill_state["q"] = []

    def emit_ffn(s):
        drip_wc(len(WC["jobs"]))
        shw = allw()
        P.op("gpsimd", lambda e: e.dma_start(
            out=sgs[:, :].rearrange("p (k o) -> p k o", k=8),
            in_=sg.rearrange("(k p) o -> p k o", p=128)),
            waits=shw, inc="dsh", inc_n=16, key=f"sg_ld_{s}")
        P.op("gpsimd", lambda e: e.dma_start(
            out=sus[:, :].rearrange("p (k o) -> p k o", k=8),
            in_=su.rearrange("(k p) o -> p k o", p=128)),
            waits=shw, inc="dsh", inc_n=16, key=f"su_ld_{s}")
        P.op("gpsimd", lambda e: e.dma_start(
            out=sds[:, :].rearrange("p (k o) -> p k o", k=4),
            in_=sd.rearrange("(k p) o -> p k o", p=128)),
            waits=shw, inc="dsh", inc_n=16, key=f"sd_ld_{s}")
        P.op("gpsimd", lambda e: e.dma_start(
            out=wos[:, :].rearrange("p (k o) -> p k o", k=8),
            in_=wo.rearrange("(k p) o -> p k o", p=128)),
            waits=shw, inc="dsh", inc_n=16, key=f"wo_ld_{s}")

        def selinit(e):
            e.memset(selw[:, :], 1.0)
            e.drain()
            return e.affine_select(out=selw[:, :], in_=selw[:, :],
                                   compare_op=OP.is_equal, fill=0.0, base=0,
                                   pattern=[[-1, NE], [0, 128]],
                                   channel_multiplier=1)
        P.op("gpsimd", selinit, waits=allw(), inc="gp", inc_n=1, key=f"selw_{s}")
        P.fence(dma=("dx0", "dx1", "dwv", "dwqk0", "dwqk1", "dsw", "dod", "dqm", "dpv"))
        for nt in range(4):
            def lgm(e, nt=nt):
                r = None
                for kc in range(8):
                    r = e.matmul(pvb[nt % 2][0:NE, :],
                                 rwsb[:, kc * NE:(kc + 1) * NE],
                                 xn[:, kc * 2048 + nt * 512: kc * 2048 + (nt + 1) * 512],
                                 start=(kc == 0), stop=(kc == 7))
                return r
            P.op("tensor", lgm, waits=["ld_rw"] +
                 ([f"lg_ev_{s}_{nt - 2}"] if nt >= 2 else []),
                 inc="pe", inc_n=1, key=f"lg_pe_{s}_{nt}")
            P.op("vector", lambda e, nt=nt: e.tensor_scalar(
                lem[0:NE, nt * 512:(nt + 1) * 512],
                pvb[nt % 2][0:NE, :],
                rbsb[0:NE, 0:1], None, op0=OP.add),
                waits=[f"lg_pe_{s}_{nt}", "ld_rb"], inc="ve", inc_n=1,
                key=f"lg_ev_{s}_{nt}")
        bank = (lambda i: pq2 if i % 2 == 0 else plb)
        for i in range(16):
            tw = [f"lg_ev_{s}_3"] + ([f"wem_{s}_{i - 2}"] if i >= 2 else [])
            P.op("tensor", lambda e, i=i, bank=bank: e.matmul(
                bank(i)[:, 0:NE], lem[0:NE, i * 128:(i + 1) * 128],
                ident[0:NE, 0:NE], start=True, stop=True),
                waits=tw, inc="pe", inc_n=1, key=f"rt_pe_{s}_{i}")
            P.op("vector", lambda e, i=i, bank=bank: e.reduce_max(
                nmf[:, 0:1], bank(i)[:, 0:NE], axis=AX, negate=True),
                waits=[f"rt_pe_{s}_{i}"] + ([f"rt_w_{s}_{i - 1}"] if i else []),
                inc="ve", inc_n=1, key=f"rt_nm_{s}_{i}")
            P.op("scalar", lambda e, i=i, bank=bank: e.activation(
                ssf[:, :], bank(i)[:, 0:NE], AF.Exp, bias=nmf[:, 0:1]),
                waits=[f"rt_nm_{s}_{i}"] + ([f"rt_w_{s}_{i - 1}"] if i else []),
                inc="sc", inc_n=1, key=f"rt_exp_{s}_{i}")

            def rmath(e, i=i):
                def nops(n=4):
                    e.drain()
                e.max(out=t8f[:, :], in_=ssf[:, :])
                nops()
                e.reduce_sum(nmf[:, 1:2], t8f[:, 0:4], axis=AX)
                e.tensor_scalar(mskf[:, :], ssf[:, :], t8f[:, 3:4], None, op0=OP.is_ge)
                nops()
                e.reciprocal(nmf[:, 2:3], nmf[:, 1:2])
                nops()
                return e.scalar_tensor_tensor(
                    wtm[:, i * NE:(i + 1) * NE], ssf[:, :], nmf[:, 2:3], mskf[:, :],
                    op0=OP.mult, op1=OP.mult)
            P.op("vector", rmath, waits=[f"rt_exp_{s}_{i}"],
                 inc="ve", inc_n=1, key=f"rt_w_{s}_{i}")
            P.op("tensor", lambda e, i=i, bank=bank: e.matmul(
                bank(i)[0:NE, 128:256], wtm[:, i * NE:(i + 1) * NE], ident[:, :],
                start=True, stop=True),
                waits=[f"rt_w_{s}_{i}"] + ([f"wem_{s}_{i - 2}"] if i >= 2 else []),
                inc="pe", inc_n=1, key=f"rt_tp_{s}_{i}")
            P.op("scalar", lambda e, i=i, bank=bank: e.copy(
                wem[0:NE, i * 128:(i + 1) * 128], bank(i)[0:NE, 128:256]),
                waits=[f"rt_tp_{s}_{i}"], inc="sc", inc_n=1, key=f"wem_{s}_{i}")
        P.fence(dma=("dx0", "dx1", "dwv", "dwqk0", "dwqk1", "dsw", "dod", "dqm", "dpv"))

        for nt in range(4):
            k = f"{s}_{nt}"
            pk = f"{s}_{nt - 1}" if nt > 0 else (f"{s - 1}_3" if s > 0 else None)
            P.op("gpsimd", lambda e, nt=nt: e.dma_start(
                out=x_bf[nt % 2][:, :].rearrange("p (a d) -> p a d", a=4),
                in_=xin[s, nt * 512:(nt + 1) * 512, :].rearrange(
                    "(a p) d -> p a d", p=128)),
                waits=([f"ds_pe_{s}_{nt - 1}_7"] if nt >= 1 else allw()),
                inc="dxb", inc_n=16, key=f"xbf_{k}")
            for hc in range(4):
                def sgm(e, hc=hc, nt=nt):
                    r = None
                    for kc in range(8):
                        r = e.matmul(ps4[:, (hc % 2) * 512:(hc % 2) * 512 + 512],
                                     sgs[:, kc * 512 + hc * 128: kc * 512 + (hc + 1) * 128],
                                     xn[:, kc * 2048 + nt * 512: kc * 2048 + (nt + 1) * 512],
                                     start=(kc == 0), stop=(kc == 7))
                    return r
                sgw = [f"wo_ld_{s}"] + ([f"ssil_{k}_{hc - 2}"] if hc >= 2 else [])
                if hc < 2 and pk:
                    sgw += [f"oev_{pk}_{hc}"]
                P.op("tensor", sgm, waits=sgw,
                     inc="pe", inc_n=1, key=f"sg_pe_{k}_{hc}")

                def sum_(e, hc=hc, nt=nt):
                    r = None
                    for kc in range(8):
                        r = e.matmul(ps4[:, 1024 + (hc % 2) * 512: 1024 + (hc % 2) * 512 + 512],
                                     sus[:, kc * 512 + hc * 128: kc * 512 + (hc + 1) * 128],
                                     xn[:, kc * 2048 + nt * 512: kc * 2048 + (nt + 1) * 512],
                                     start=(kc == 0), stop=(kc == 7))
                    return r
                suw = [f"wo_ld_{s}"] + ([f"shm_{k}_{hc - 2}"] if hc >= 2 else [])
                if hc < 2 and pk:
                    suw += [f"oev_{pk}_{2 + hc}"]
                P.op("tensor", sum_, waits=suw,
                     inc="pe", inc_n=1, key=f"su_pe_{k}_{hc}")
                slw = [f"sg_pe_{k}_{hc}"] + ([f"shm_{k}_{hc - 1}"] if hc else [])
                if hc == 0 and pk:
                    slw += [f"tm_{pk}_31"]
                P.op("scalar", lambda e, hc=hc: e.activation(
                    gs_sb[:, :], ps4[:, (hc % 2) * 512:(hc % 2) * 512 + 512], AF.Silu),
                    waits=slw, inc="sc", inc_n=1, key=f"ssil_{k}_{hc}")
                P.op("vector", lambda e, hc=hc: e.tensor_mul(
                    sh_sb[:, hc * 512:(hc + 1) * 512], gs_sb[:, :],
                    ps4[:, 1024 + (hc % 2) * 512: 1024 + (hc % 2) * 512 + 512]),
                    waits=[f"ssil_{k}_{hc}", f"su_pe_{k}_{hc}"],
                    inc="ve", inc_n=1, key=f"shm_{k}_{hc}")
            for ei in range(NE):
                gu = gu_sb[ei % 3]
                lw = []
                if ei >= 3:
                    lw += [f"u_pe_{k}_{ei - 3}"]
                elif pk:
                    lw += [f"u_pe_{pk}_{NE - 3 + ei}"]
                elif nt == 0:
                    lw += allw()
                P.op("sync", lambda e, gu=gu, ei=ei: e.dma_start(
                    out=gu[:, 0:1024], in_=gwc[ei]),
                    waits=lw + [f"wc_st_{3 * ei}"],
                    inc=f"dgu{ei % 3}", inc_n=16, key=f"g_ld_{k}_{ei}")
                P.op("sync", lambda e, gu=gu, ei=ei: e.dma_start(
                    out=gu[:, 1024:2048], in_=uwc[ei]),
                    waits=lw + [f"wc_st_{3 * ei + 1}"],
                    inc=f"dgu{ei % 3}", inc_n=16, key=f"u_ld_{k}_{ei}")
                wbank = pq2 if ei % 2 == 0 else plb
                wpw = [f"selw_{s}"] + ([f"wbc_ev_{k}_{ei - 2}"] if ei >= 2 else [])
                if ei < 2 and pk:
                    wpw += [f"oev_{pk}_{6 + ei}"]
                P.op("tensor", lambda e, ei=ei, nt=nt, wbank=wbank: e.matmul(
                    wbank[:, :],
                    selw[0:NE, ei * 128:(ei + 1) * 128],
                    wem[0:NE, nt * 512:(nt + 1) * 512], start=True, stop=True),
                    waits=wpw, inc="pe", inc_n=1, key=f"wbc_pe_{k}_{ei}")
                P.op("scalar", lambda e, ei=ei, wbank=wbank: e.copy(
                    wbc_sb[ei % 2][:, :], wbank[:, :]),
                    waits=[f"wbc_pe_{k}_{ei}"] +
                          ([f"hm_{k}_{ei - 2}"] if ei >= 2 else
                           ([f"hm_{pk}_{30 + ei}"] if pk else [])),
                    inc="sc", inc_n=1, key=f"wbc_ev_{k}_{ei}")

                gb = [ps4[:, 0:512], ps4[:, 512:1024], pvb[0][:, :]][ei % 3]
                ub = [ps4[:, 1024:1536], ps4[:, 1536:2048], pvb[1][:, :]][ei % 3]

                def gmm(e, gu=gu, nt=nt, ei=ei, gb=gb):
                    r = None
                    for kc in range(8):
                        r = e.matmul(gb, gu[:, kc * 128:(kc + 1) * 128],
                                     xn[:, kc * 2048 + nt * 512: kc * 2048 + (nt + 1) * 512],
                                     start=(kc == 0), stop=(kc == 7))
                    return r
                gw_ = [f"u_ld_{k}_{ei}"]
                if ei >= 3:
                    gw_ += [f"gsil_{k}_{ei - 3}"]
                elif ei % 3 < 2:
                    gw_ += [f"ssil_{k}_{(ei % 3) + 2}"]
                elif pk:
                    gw_ += [f"oev_{pk}_4"]
                P.op("tensor", gmm, waits=gw_, inc="pe", inc_n=1, key=f"g_pe_{k}_{ei}")

                def umm(e, gu=gu, nt=nt, ei=ei, ub=ub):
                    r = None
                    for kc in range(8):
                        r = e.matmul(ub, gu[:, 1024 + kc * 128: 1024 + (kc + 1) * 128],
                                     xn[:, kc * 2048 + nt * 512: kc * 2048 + (nt + 1) * 512],
                                     start=(kc == 0), stop=(kc == 7))
                    return r
                uw_ = [f"u_ld_{k}_{ei}"]
                if ei >= 3:
                    uw_ += [f"tm_{k}_{ei - 3}"]
                elif ei % 3 < 2:
                    uw_ += [f"shm_{k}_{(ei % 3) + 2}"]
                elif pk:
                    uw_ += [f"oev_{pk}_5"]
                P.op("tensor", umm, waits=uw_, inc="pe", inc_n=1, key=f"u_pe_{k}_{ei}")
                P.op("scalar", lambda e, ei=ei, gb=gb: e.activation(
                    gs_sb[:, :], gb, AF.Silu),
                    waits=[f"g_pe_{k}_{ei}"] +
                          ([f"tm_{k}_{ei - 1}"] if ei else [f"shm_{k}_3"]),
                    inc="sc", inc_n=1, key=f"gsil_{k}_{ei}")
                tmw = [f"gsil_{k}_{ei}", f"u_pe_{k}_{ei}"]
                if ei:
                    tmw += [f"hm_{k}_{ei - 1}"]
                elif pk:
                    tmw += [f"hm_{pk}_31"]
                P.op("vector", lambda e, ei=ei, ub=ub: e.tensor_mul(
                    tt_sb[:, :], gs_sb[:, :], ub),
                    waits=tmw, inc="ve", inc_n=1, key=f"tm_{k}_{ei}")
                hw = [f"tm_{k}_{ei}", f"wbc_ev_{k}_{ei}"]
                P.op("vector", lambda e, ei=ei: e.tensor_mul(
                    h_sb[:, ei * 512:(ei + 1) * 512], tt_sb[:, :], wbc_sb[ei % 2][:, :]),
                    waits=hw, inc="ve", inc_n=1, key=f"hm_{k}_{ei}")
            def dbank(dc, lo=0, hi=512):
                if dc < 4:
                    return ps4[:, dc * 512 + lo: dc * 512 + hi]
                if dc < 6:
                    return pvb[dc - 4][:, lo:hi]
                return (pq2 if dc == 6 else plb)[:, lo:hi]
            bank_free = {0: f"gsil_{k}_30", 1: f"gsil_{k}_31",
                         2: f"tm_{k}_30", 3: f"tm_{k}_31",
                         4: f"gsil_{k}_29", 5: f"tm_{k}_29",
                         6: f"wbc_ev_{k}_30", 7: f"wbc_ev_{k}_31"}
            for dc in range(8):
                def womm(e, dc=dc, nt=nt):
                    r = None
                    for kc in range(8):
                        r = e.matmul(dbank(dc, 0, 512),
                                     wos[:, kc * 1024 + dc * 128: kc * 1024 + (dc + 1) * 128],
                                     pv[:, kc * 2048 + nt * 512: kc * 2048 + (nt + 1) * 512],
                                     start=(kc == 0), stop=False,
                                     skip_group_check=True)
                    return r
                ww = [f"wo_ld_{s}", f"pvdma_{s}_15", bank_free[dc]]
                if nt > 0:
                    ww += [f"oev_{s}_{nt - 1}_{dc}"]
                elif s > 0:
                    ww += [f"oev_{s - 1}_3_{dc}"]
                P.op("tensor", womm, waits=ww, inc="pe", inc_n=1, key=f"wo_pe_{k}_{dc}")
            for ei in range(NE):
                dwb = dw_sb[ei % 3]
                lw = [f"wc_st_{3 * ei + 2}"]
                if ei >= 3:
                    lw += [f"dse_{k}_{ei - 3}"]
                elif pk:
                    lw += [f"dse_{pk}_{NE - 3 + ei}"]
                elif nt == 0:
                    lw += allw()
                P.op("sync", lambda e, dwb=dwb, ei=ei: e.dma_start(
                    out=dwb[:, :], in_=dwc[ei]),
                    waits=lw, inc=f"ddw{ei % 3}", inc_n=16, key=f"d_ld_{k}_{ei}")
                for dc in range(8):
                    P.op("tensor", lambda e, dc=dc, ei=ei, dwb=dwb: e.matmul(
                        dbank(dc, 0, 512), dwb[:, dc * 128:(dc + 1) * 128],
                        h_sb[:, ei * 512:(ei + 1) * 512], start=False, stop=False,
                        skip_group_check=True),
                        waits=([f"d_ld_{k}_{ei}", f"hm_{k}_{ei}"] if dc == 0 else []),
                        inc="pe", inc_n=1,
                        key=(f"dse_{k}_{ei}" if dc == 7 else None))
            for dc in range(8):
                for a in range(4):
                    P.op("tensor", lambda e, dc=dc, a=a, nt=nt: e.matmul(
                        dbank(dc, a * 128, (a + 1) * 128),
                        x_bf[nt % 2][:, a * 1024 + dc * 128: a * 1024 + (dc + 1) * 128],
                        ident[:, :], start=False, stop=False,
                        skip_group_check=True),
                        waits=([f"xbf_{k}"] if (dc == 0 and a == 0) else []),
                        inc="pe", inc_n=1)
            for hc in range(4):
                for dc in range(8):
                    P.op("tensor", lambda e, dc=dc, hc=hc: e.matmul(
                        dbank(dc, 0, 512), sds[:, hc * 1024 + dc * 128: hc * 1024 + (dc + 1) * 128],
                        sh_sb[:, hc * 512:(hc + 1) * 512], start=False,
                        stop=(hc == 3), skip_group_check=True),
                        waits=([f"wo_ld_{s}", f"shm_{k}_{hc}"] if dc == 0 else []),
                        inc="pe", inc_n=1,
                        key=(f"ds_pe_{k}_{dc}" if hc == 3 else None))
            for dc in range(8):
                ow = [f"ds_pe_{k}_{dc}"]
                if dc >= 2:
                    ow += [f"ost_{k}_{dc - 2}"]
                elif pk:
                    ow += [f"ost_{pk}_{dc + 6}"]
                P.op("vector", lambda e, dc=dc: e.tensor_copy(
                    out_sb[dc % 2][:, :], dbank(dc, 0, 512)),
                    waits=ow, inc="ve", inc_n=1, key=f"oev_{k}_{dc}")
                dma("sync", y[dc * 128:(dc + 1) * 128,
                              s * 2048 + nt * 512: s * 2048 + (nt + 1) * 512],
                    out_sb[dc % 2][:, :], waits=[f"oev_{k}_{dc}"],
                    sem=f"dout{dc % 2}", key=f"ost_{k}_{dc}")
        P.marker(f"b4_pe_done_{s}", "pe")

    dumps = []

    def dbg_dump(tag, t, rows, cols, dtype):
        d = nc.dram_tensor(f"dbg_{tag}", [rows, cols], dtype,
                           kind="ExternalOutput").ap()
        w = [(sm_, c) for sm_, c in P.cnt.items() if not sm_.startswith("_")]
        dma("sync", d, t[0:rows, 0:cols], waits=w, key=f"dbg_{tag}", sem="ddbg")
        dumps.append(tag)

    stop = DBG.get("stop")
    emit_init()
    done = False
    for s in range(SEQ):
        if done:
            break
        P.fence()
        emit_ln(s)
        if stop == f"ln{s}":
            dbg_dump(f"xn{s}", xn, 128, 8 * 2048, BF16)
            dbg_dump(f"bn{s}", bn6, 128, 16, F32)
            dbg_dump(f"mv{s}", mv, 128, 8, F32)
            dbg_dump(f"xsb{s}", x_sb[1], 128, 1024, BF16)
            done = True
            break
        P.fence()
        for g in range(2):
            emit_attn_group(s, g)
            if stop == f"attn{s}{g}":
                dbg_dump(f"vsb{s}{g}", v_sb, 128, 16 * 1040, BF16)
                dbg_dump(f"pv{s}{g}", pv, 128, 8 * 2048, BF16)
                done = True
                break
        if done:
            break
        if stop == f"rt{s}":
            emit_ffn(s)
        emit_ffn(s)
        if stop == f"ffn{s}":
            done = True

    DMA_SEMS = ["dini", "dinig", "dcs", "dwc0", "dwc1", "dwc2", "dws0", "dws1",
                "dws2", "dx0", "dx1", "dwv", "dwqk0",
                "dwqk1", "dsw", "dod", "dqm", "dpv", "dsh", "dxb", "dgu0",
                "dgu1", "dgu2", "ddw0", "ddw1", "ddw2",
                "dout0", "dout1", "ddbg", "dx", "dg"]
    P.op("sync", lambda e: None,
         waits=[(sm_, P.cnt.get(sm_, 0)) for sm_ in DMA_SEMS])

    from contextlib import ExitStack
    with ExitStack() as ctx:
        all_sems = ["pe", "sc", "ve", "gp", "dx", "dg", "dini", "dinig", "dcs",
                    "dwc0", "dwc1", "dwc2", "dws0", "dws1", "dws2",
                    "dx0", "dx1", "dx2", "dx3", "dwv",
                    "dwqk0", "dwqk1", "dsw", "dod", "dqm", "dpv", "dsh",
                    "dxb", "dgu0", "dgu1", "dgu2", "dgu3",
                    "ddw0", "ddw1", "ddw2", "ddw3", "dout0", "dout1", "ddbg"]
        semh = {nm: ctx.enter_context(nc.semaphore(nm)) for nm in all_sems}
        with nc.allow_low_precision(reason="bf16 matmul operand staging; "
                                     "fp32 accumulation kept in PSUM"), \
             nc.Block() as block:
            block.sync(lambda e: P.emit("sync", e, semh))
            block.gpsimd(lambda e: P.emit("gpsimd", e, semh))
            block.tensor(lambda e: P.emit("tensor", e, semh))
            block.scalar(lambda e: P.emit("scalar", e, semh))
            block.vector(lambda e: P.emit("vector", e, semh))
    return nc


def _rope_tables():
    dh = 64
    inv = 1.0 / (10000.0 ** (np.arange(0, dh, 2, dtype=np.float64) / dh))
    t = np.arange(S, dtype=np.float64)
    fr = np.outer(inv, t)
    emb = np.concatenate([fr, fr], axis=0)
    cos = np.cos(emb)
    sin = np.sin(emb)
    sgn = np.ones((dh, 1))
    sgn[: dh // 2] = -1.0
    sin = sin * sgn
    cos2 = np.concatenate([cos, cos], axis=0)
    sin2 = np.concatenate([sin, sin], axis=0)
    chi = cos2.astype(np.float32)
    shi = sin2.astype(np.float32)
    clo = (cos2 - chi.astype(np.float64)).astype(np.float32)
    slo = (sin2 - shi.astype(np.float64)).astype(np.float32)
    return (np.ascontiguousarray(chi), np.ascontiguousarray(shi),
            np.ascontiguousarray(clo), np.ascontiguousarray(slo))


_NC_CACHE = {}


def kernel(L0_attn_x, L0_attn_Wq, L0_attn_Wk, L0_attn_Wv, L0_attn_Wo,
           L0_ffn_router_w, L0_ffn_router_bias, L0_ffn_gate_w, L0_ffn_up_w,
           L0_ffn_down_w, L0_ffn_sgate_w, L0_ffn_sup_w, L0_ffn_sdown_w):
    x = np.asarray(L0_attn_x)
    cos2 = _rope_tables()
    com = {
        "wq_t": np.ascontiguousarray(np.asarray(L0_attn_Wq).T),
        "wk_t": np.ascontiguousarray(np.asarray(L0_attn_Wk).T),
        "wv_t": np.ascontiguousarray(np.asarray(L0_attn_Wv).T),
        "wo_t": np.ascontiguousarray(np.asarray(L0_attn_Wo).T),
        "rw_t": np.ascontiguousarray(np.asarray(L0_ffn_router_w).T),
        "rbias": np.ascontiguousarray(np.asarray(L0_ffn_router_bias).reshape(NE, 1)),
        "gw_t": np.ascontiguousarray(np.asarray(L0_ffn_gate_w).transpose(0, 2, 1)),
        "uw_t": np.ascontiguousarray(np.asarray(L0_ffn_up_w).transpose(0, 2, 1)),
        "dw_t": np.ascontiguousarray(np.asarray(L0_ffn_down_w).transpose(0, 2, 1)),
        "sg_t": np.ascontiguousarray(np.asarray(L0_ffn_sgate_w).T),
        "su_t": np.ascontiguousarray(np.asarray(L0_ffn_sup_w).T),
        "sd_t": np.ascontiguousarray(np.asarray(L0_ffn_sdown_w).T),
        "cos_t": cos2[0],
        "sin_t": cos2[1],
        "cos_lo": cos2[2],
        "sin_lo": cos2[3],
    }
    in_maps = []
    for c in range(NCORES):
        m = dict(com)
        m["x"] = np.ascontiguousarray(x[SEQ * c: SEQ * (c + 1)])
        in_maps.append(m)

    if "nc" not in _NC_CACHE:
        _NC_CACHE["nc"] = build_nc()
    nc = _NC_CACHE["nc"]
    res = run_bass_kernel_spmd(nc, in_maps, core_ids=list(range(NCORES)))
    outs = []
    for c in range(NCORES):
        yc = res.results[c]["y"]
        outs.append(np.ascontiguousarray(np.asarray(yc).T).reshape(SEQ, S, D))
    return np.concatenate(outs, axis=0)
